# version 09
# baseline (speedup 1.0000x reference)
import numpy as np
from contextlib import ExitStack

import concourse.bass as bass
import concourse.mybir as mybir
from concourse.bass_utils import run_bass_kernel_spmd

F32 = mybir.dt.float32
BF16 = mybir.dt.bfloat16
I32 = mybir.dt.int32
AF = mybir.ActivationFunctionType
ALU = mybir.AluOpType
AXX = mybir.AxisListType.X

D = 2048
S = 2048
INTER = 5504
T = 1024
KT = 16
IT = 43
EPS = 1e-5
DCHUNK = 4

XN2_N = 16384
HALL_N = 16384
BF_N = 6144
XNT_N = 4096
TMP_N = 6144
IDXN = 2048


def build_nc():
    es = ExitStack()
    nc = bass.Bass("TRN2", target_bir_lowering=False)

    x_in = nc.dram_tensor("x_in", [8, 128, D], F32, kind="ExternalInput").ap()
    wT = {}
    for nm in ["wf", "wzg", "wz", "wga", "wgb", "wo"]:
        wT[nm] = nc.dram_tensor(nm + "T", [KT, 128, D], F32, kind="ExternalInput").ap()
    smalls_in = nc.dram_tensor("smalls", [128, 128], F32, kind="ExternalInput").ap()
    lut_in = nc.dram_tensor("lut128", [128, 16], F32, kind="ExternalInput").ap()
    ident_in = nc.dram_tensor("ident", [128, 128], F32, kind="ExternalInput").ap()
    gidx_in = nc.dram_tensor("gidx", [2, 128, INTER], I32, kind="ExternalInput").ap()
    uidx_in = nc.dram_tensor("uidx", [2, 128, INTER], I32, kind="ExternalInput").ap()
    didx_in = nc.dram_tensor("didx", [IT, 128, 256], I32, kind="ExternalInput").ap()
    out_dram = nc.dram_tensor("out", [8, 128, D], F32, kind="ExternalOutput").ap()

    xnsp = nc.dram_tensor("xnsp", [2, KT, 128, T], BF16, kind="Internal").ap()
    ysp = nc.dram_tensor("ysp", [KT, 128, T], F32, kind="Internal").ap()
    stg = nc.dram_tensor("stg", [3, 86, 2, 128, 128], BF16, kind="Internal").ap()
    agg = nc.dram_tensor("agg", [3, 8, 86, 2, 128, 128], BF16, kind="Internal",
                     addr_space="Shared").ap()
    bnd_in = nc.dram_tensor("bnd_in", [128, 16], BF16, kind="Internal").ap()
    bnd_out = nc.dram_tensor("bnd_out", [2, 128, 16], BF16, kind="Internal").ap()

    with es:
        AR_XN2 = es.enter_context(nc.sbuf_tensor("AR_XN2", [128, XN2_N], F32))
        AR_HALL = es.enter_context(nc.sbuf_tensor("AR_HALL", [128, HALL_N], F32))
        AR_BF = es.enter_context(nc.sbuf_tensor("AR_BF", [128, BF_N], F32))
        AR_XNT = es.enter_context(nc.sbuf_tensor("AR_XNT", [128, XNT_N], F32))
        AR_TMP = es.enter_context(nc.sbuf_tensor("AR_TMP", [128, TMP_N], F32))
        AR_IDX = es.enter_context(nc.sbuf_tensor("AR_IDX", [128, IDXN], I32))
        smalls = es.enter_context(nc.sbuf_tensor("smalls_sb", [128, 128], F32))
        lutb = es.enter_context(nc.sbuf_tensor("lutb", [128, 128], F32))
        luthi = es.enter_context(nc.sbuf_tensor("luthi", [128, 128], F32))
        lutlo = es.enter_context(nc.sbuf_tensor("lutlo", [128, 128], F32))
        identsb = es.enter_context(nc.sbuf_tensor("identsb", [128, 128], F32))
        stats = es.enter_context(nc.sbuf_tensor("stats", [128, 256], F32))
        zerob = es.enter_context(nc.sbuf_tensor("zerob", [128, 1024], BF16))
        bndsb = es.enter_context(nc.sbuf_tensor("bndsb", [128, 256], BF16))
        warm = es.enter_context(nc.sbuf_tensor("warm", [128, 256], F32))
        psum = es.enter_context(nc.psum_tensor("psum", [128, 4096], F32))

        sem_q1 = es.enter_context(nc.semaphore("q1"))
        sem_qg = es.enter_context(nc.semaphore("qg"))
        sem_pe = es.enter_context(nc.semaphore("pe"))
        sem_dv = es.enter_context(nc.semaphore("dv"))
        sem_ac = es.enter_context(nc.semaphore("ac"))
        sem_cc = es.enter_context(nc.semaphore("cc"))

        bfBF = AR_BF[:].bitcast(BF16)
        bfXNT = AR_XNT[:].bitcast(BF16)
        bfXN2 = AR_XN2[:].bitcast(BF16)
        bfHALL = AR_HALL[:].bitcast(BF16)

        prog = {k: [] for k in ["sync", "gpsimd", "vector", "scalar", "tensor"]}
        cnt = {"q1": 0, "qg": 0, "pe": 0, "dv": 0, "ac": 0, "cc": 0}
        sems = {"q1": sem_q1, "qg": sem_qg, "pe": sem_pe, "dv": sem_dv,
                "ac": sem_ac, "cc": sem_cc}

        def emit(eng, fn):
            prog[eng].append(fn)

        def op(eng, semname, emit_fn, k=1, waits=()):
            for wsem, wval in waits:
                wh, wv = sems[wsem], int(wval)
                if wv > 0:
                    emit(eng, (lambda e, wh=wh, wv=wv: e.wait_ge(wh, wv)))
            cnt[semname] += k
            sh = sems[semname]
            emit(eng, (lambda e, fn=emit_fn, sh=sh, k=k: fn(e).then_inc(sh, k)))
            return cnt[semname]

        def wait(eng, semname, value):
            wh, wv = sems[semname], int(value)
            emit(eng, (lambda e, wh=wh, wv=wv: e.wait_ge(wh, wv)))

        bank_state = [[] for _ in range(8)]
        bank_rr = [0]

        def bank_acquire():
            b = bank_rr[0] % 8
            bank_rr[0] += 1
            w = bank_state[b]
            bank_state[b] = []
            return b, list(w)

        def bank_release(b, semname, value):
            bank_state[b].append((semname, value))

        def pslot(b):
            return psum[:, 512 * b:512 * (b + 1)]

        SM_RMS, SM_WFB, SM_WOB, SM_MSK, SM_EPS, SM_INVD = 0, 16, 32, 48, 64, 80

        op("sync", "q1", lambda e: e.dma_start(out=smalls[:], in_=smalls_in), k=16)
        op("sync", "q1", lambda e: e.dma_start(out=lutb[:, 0:16], in_=lut_in), k=16)
        op("sync", "q1", lambda e: e.dma_start(out=identsb[:], in_=ident_in), k=16)
        q1_init = cnt["q1"]

        def dve_warm(e):
            wb = warm[:].bitcast(BF16)
            e.memset(warm[:], 0.0)
            e.memset(wb[:, 0:256], 0.0)
            for w0, w1 in [(warm[:, 0:128], warm[:, 128:256]),
                           (wb[:, 0:128], wb[:, 128:256])]:
                e.tensor_scalar(out=w0, in0=w0, scalar1=1.0, scalar2=None,
                                op0=ALU.is_equal)
                e.scalar_tensor_tensor(out=w0, in0=w0, scalar=1.0, in1=w0,
                                       op0=ALU.mult, op1=ALU.add)
                e.tensor_copy(w1, w0)
                e.tensor_tensor(w0, w0, w1, ALU.subtract)
            e.tensor_copy(wb[:, 256:384], warm[:, 64:128].bitcast(BF16))
            e.tensor_copy(warm[:, 64:128], wb[:, 0:64])
            e.tensor_tensor(wb[:, 0:128], warm[:, 0:64].bitcast(BF16),
                            wb[:, 256:384], ALU.subtract)
            e.tensor_tensor_scan(wb[:, 0:64], wb[:, 64:128], wb[:, 128:192], 0.0,
                                 ALU.mult, ALU.add)
            e.tensor_reduce(warm[:, 128:129], warm[:, 0:64], AXX, ALU.add)
            e.reciprocal(warm[:, 130:131], warm[:, 129:130])
            e.memset(zerob[:], 0.0)
            return e.memset(warm[:], 0.0)
        dv_warm = op("vector", "dv", dve_warm)

        wait("vector", "q1", 32)

        def lut_prep(e):
            lbh = luthi[:, 64:128].bitcast(BF16)
            e.tensor_copy(lbh[:, 0:16], lutb[:, 0:16])
            e.memset(warm[:], 0.0)
            e.memset(warm[:], 0.0)
            e.tensor_copy(luthi[:, 0:16], lbh[:, 0:16])
            e.memset(warm[:], 0.0)
            e.memset(warm[:], 0.0)
            llh = lutlo[:, 64:128].bitcast(BF16)
            e.tensor_tensor(llh[:, 0:16], lutb[:, 0:16], luthi[:, 0:16], ALU.subtract)
            e.memset(warm[:], 0.0)
            e.memset(warm[:], 0.0)
            return e.tensor_copy(lutlo[:, 0:16], llh[:, 0:16])
        dv_lut = op("vector", "dv", lut_prep)


        def act_warm(e):
            e.activation(out=warm[:, 0:128], in_=warm[:, 128:256], func=AF.Square)
            return e.activation(out=warm[:, 0:128], in_=warm[:, 128:256], func=AF.Sqrt)
        op("scalar", "ac", act_warm, waits=[("dv", dv_warm)])

        gchunks = []
        for arr, src in [(0, gidx_in), (1, uidx_in)]:
            for ti in range(2):
                j = 0
                while j < IT:
                    n = min(8, IT - j)
                    gchunks.append(("gu", arr, src, ti, j, n))
                    j += n
        j = 0
        while j < IT:
            n = min(4, IT - j)
            gchunks.append(("dn", 2, didx_in, 0, j, n))
            j += n
        GSLOT = [0, 4096]
        cc_arr = {}
        f32XNT = AR_XNT

        class Gather:
            def __init__(self):
                self.gi = 0
                self.k = -1
                self.loaded = 0
                self.done = False

            def _load(self, gi):
                kind, arr, srcA, ti, j0, n = gchunks[gi]
                ncols = n * 128 if kind == "gu" else n * 256
                slot = gi % 2
                idxs = AR_IDX[:, 1024 * slot:1024 * slot + ncols]
                w = []
                if gi >= 2:
                    w.append(("dv", chunk_dv[gi - 2]))

                if kind == "gu":
                    sap = srcA[ti, :, 128 * j0:128 * j0 + ncols]
                    op("gpsimd", "qg", lambda e, idxs=idxs, sap=sap:
                       e.dma_start(out=idxs, in_=sap), k=16, waits=w)
                else:
                    sap = srcA[j0:j0 + n].rearrange("i p q -> p i q")
                    dst3 = idxs.rearrange("p (i q) -> p i q", i=n)
                    op("gpsimd", "qg", lambda e, dst3=dst3, sap=sap:
                       e.dma_start(out=dst3, in_=sap), k=16, waits=w)
                chunk_ldq[gi] = cnt["qg"]
                self.loaded = gi + 1

            def step(self):
                if self.done:
                    return False
                gi = self.gi
                if gi >= len(gchunks):
                    self.done = True
                    return False
                kind, arr, srcA, ti, j0, n = gchunks[gi]
                ncols = n * 128 if kind == "gu" else n * 256
                slot = gi % 2
                base = GSLOT[slot]
                idxs = AR_IDX[:, 1024 * slot:1024 * slot + ncols]
                idxb = bfXNT[:, base:base + ncols]
                acc32 = f32XNT[:, (base + 1024) // 2:(base + 1024) // 2 + ncols]
                mask = bfXNT[:, base + 3072:base + 3072 + ncols]
                if self.k == -1:
                    while self.loaded < min(gi + 2, len(gchunks)):
                        self._load(self.loaded)

                    def cast(e, idxs=idxs, idxb=idxb, acc32=acc32):
                        e.tensor_copy(idxb, idxs)
                        return e.memset(acc32, 0.0)
                    op("vector", "dv", cast,
                       waits=[("qg", chunk_ldq[gi]), ("dv", dv_lut)])
                    self.k = 0
                    return True
                if self.k < 16:
                    k_ = self.k

                    def trip(e, idxb=idxb, acc32=acc32, mask=mask, k_=k_):
                        e.tensor_scalar(out=mask, in0=idxb, scalar1=float(k_),
                                        scalar2=None, op0=ALU.is_equal)
                        return e.scalar_tensor_tensor(
                            out=acc32, in0=mask, scalar=lutb[:, k_:k_ + 1],
                            in1=acc32, op0=ALU.mult, op1=ALU.add)
                    dvv = op("vector", "dv", trip)
                    self.k += 1
                    if self.k == 16:
                        self._acc_dv = dvv
                    return True
                ahs = op("scalar", "ac", lambda e, acc32=acc32, mask=mask:
                         e.activation(out=mask, in_=acc32, func=AF.Copy),
                         waits=[("dv", self._acc_dv)])
                dlo = op("vector", "dv", lambda e, acc32=acc32, mask=mask, idxb=idxb:
                         e.scalar_tensor_tensor(out=idxb, in0=acc32, scalar=1.0,
                                                in1=mask, op0=ALU.mult,
                                                op1=ALU.subtract),
                         waits=[("ac", ahs)])
                chunk_dv[gi] = dlo
                if kind == "gu":
                    dsth = stg[arr, ti * IT + j0:ti * IT + j0 + n, 0].rearrange(
                        "j p q -> p j q")
                    dstl = stg[arr, ti * IT + j0:ti * IT + j0 + n, 1].rearrange(
                        "j p q -> p j q")
                    srch = mask.rearrange("p (j q) -> p j q", j=n)
                    srcl = idxb.rearrange("p (j q) -> p j q", j=n)
                else:
                    dsth = stg[arr, 2 * j0:2 * (j0 + n), 0].rearrange("i p q -> p i q")
                    dstl = stg[arr, 2 * j0:2 * (j0 + n), 1].rearrange("i p q -> p i q")
                    srch = mask.rearrange("p (i q) -> p i q", i=2 * n)
                    srcl = idxb.rearrange("p (i q) -> p i q", i=2 * n)
                op("gpsimd", "qg", lambda e, d=dsth, s=srch:
                   e.dma_start(out=d, in_=s), k=16, waits=[("dv", dlo)])
                stq = op("gpsimd", "qg", lambda e, d=dstl, s=srcl:
                         e.dma_start(out=d, in_=s), k=16)
                self.gi += 1
                self.k = -1
                if gi + 1 == len(gchunks) or gchunks[gi + 1][1] != arr:
                    cc_arr[arr] = op("gpsimd", "cc", lambda e, arr=arr:
                                     e.collective_compute(
                                         "AllGather", ALU.bypass,
                                         replica_groups=[[0, 1, 2, 3, 4, 5, 6, 7]],
                                         ins=[stg[arr]], outs=[agg[arr]]),
                                     waits=[("qg", stq)])
                return True

        chunk_dv = {}
        chunk_ldq = {}
        gat = Gather()

        def drip_gather(nmax):
            while nmax > 0 and gat.step():
                nmax -= 1

        xn2hi = bfXN2[:, 0:16384]
        xn2lo = bfXN2[:, 16384:32768]
        tmpA = AR_HALL[:, 0:2048]
        tmpB = AR_TMP[:, 4096:6144]
        SUMS = stats[:, 0:16]
        SQS = stats[:, 16:32]
        MEAN = stats[:, 32:48]
        MSQ = stats[:, 48:64]
        VAR = stats[:, 64:80]
        VPE = stats[:, 80:96]
        RINV = stats[:, 96:112]
        EX2 = stats[:, 112:128]
        RSTD = stats[:, 128:144]
        M2 = stats[:, 144:160]
        M2E = stats[:, 160:176]
        R2INV = stats[:, 176:192]
        RST2 = stats[:, 192:208]
        EPS16 = smalls[:, SM_EPS:SM_EPS + 16]
        INVD16 = smalls[:, SM_INVD:SM_INVD + 16]

        last_sq_ac = 0
        d_red = 0
        for tt in range(8):
            xs = AR_TMP[:, 2048 * (tt % 2):2048 * (tt % 2) + 2048]
            w = [("dv", d_red)] if tt >= 2 else []
            xq = op("sync", "q1", lambda e, xs=xs, tt=tt:
                    e.dma_start(out=xs, in_=x_in[tt]), k=16, waits=w)
            wA = [("q1", xq)]
            if tt >= 1:
                wA.append(("dv", d_red))
            last_sq_ac = op("scalar", "ac", lambda e, xs=xs:
                            e.activation(out=tmpB, in_=xs, func=AF.Square), waits=wA)

            def reds(e, xs=xs, tt=tt):
                e.tensor_reduce(SUMS[:, tt:tt + 1], xs, AXX, ALU.add)
                return e.tensor_reduce(SQS[:, tt:tt + 1], tmpB, AXX, ALU.add)
            d_red = op("vector", "dv", reds,
                       waits=[("q1", xq), ("ac", last_sq_ac)])
            drip_gather(12)

        def widestats1(e):
            e.tensor_tensor(MEAN, SUMS, INVD16, ALU.mult)
            e.memset(warm[:], 0.0)
            e.tensor_tensor(EX2, SQS, INVD16, ALU.mult)
            e.memset(warm[:], 0.0)
            e.tensor_tensor(MSQ, MEAN, MEAN, ALU.mult)
            e.memset(warm[:], 0.0)
            e.tensor_tensor(VAR, EX2, MSQ, ALU.subtract)
            e.memset(warm[:], 0.0)
            e.tensor_tensor(VPE, VAR, EPS16, ALU.add)
            e.memset(warm[:], 0.0)
            return e.reciprocal(RINV, VPE)
        dws1 = op("vector", "dv", widestats1, waits=[("dv", d_red)])
        a_rstd = op("scalar", "ac", lambda e:
                    e.activation(out=RSTD, in_=RINV, func=AF.Sqrt),
                    waits=[("dv", dws1)])

        def widestats2(e):
            e.tensor_tensor(M2, VAR, RINV, ALU.mult)
            e.memset(warm[:], 0.0)
            e.tensor_tensor(M2E, M2, EPS16, ALU.add)
            e.memset(warm[:], 0.0)
            return e.reciprocal(R2INV, M2E)
        dws2 = op("vector", "dv", widestats2, waits=[("ac", a_rstd)])
        a_rst2 = op("scalar", "ac", lambda e:
                    e.activation(out=RST2, in_=R2INV, func=AF.Sqrt),
                    waits=[("dv", dws2)])

        pe_after_tt = {}
        spill_qg_prev = 0
        for quarter in range(4):
            for tloc in range(2):
                tt = quarter * 2 + tloc
                xs = AR_TMP[:, 2048 * (tt % 2):2048 * (tt % 2) + 2048]
                w = [("ac", a_rst2)]
                if tt >= 2:
                    w.append(("pe", pe_after_tt[tt - 2]))
                if tloc == 0 and quarter > 0:
                    w.append(("qg", spill_qg_prev))
                xq = op("sync", "q1", lambda e, xs=xs, tt=tt:
                        e.dma_start(out=xs, in_=x_in[tt]), k=16, waits=w)

                def normalize(e, xs=xs, tt=tt):
                    e.tensor_scalar(out=tmpB, in0=xs, scalar1=MEAN[:, tt:tt + 1],
                                    scalar2=None, op0=ALU.subtract)
                    return e.tensor_scalar(out=tmpA, in0=tmpB,
                                           scalar1=RSTD[:, tt:tt + 1], scalar2=None,
                                           op0=ALU.mult)
                wN = [("q1", xq)]
                if tt >= 1:
                    wN.append(("pe", pe_after_tt[tt - 1]))
                d3 = op("vector", "dv", normalize, waits=wN)
                d5 = op("vector", "dv", lambda e, tt=tt:
                        e.tensor_scalar(out=tmpB, in0=tmpA,
                                        scalar1=RST2[:, tt:tt + 1], scalar2=None,
                                        op0=ALU.mult),
                        waits=[("dv", d3)])

                for dti in range(KT):
                    bxn, wxn = bank_acquire()
                    pexn = op("tensor", "pe", lambda e, b=bxn, dti=dti:
                              e.transpose(pslot(b)[:, 0:128],
                                          tmpA[:, 128 * dti:128 * dti + 128],
                                          identsb[:]),
                              waits=wxn + [("dv", d3), ("q1", q1_init)])
                    hq = bfBF[:, dti * 256 + 128 * tloc:dti * 256 + 128 * tloc + 128]
                    lq = bfBF[:, 4096 + dti * 256 + 128 * tloc:
                               4096 + dti * 256 + 128 * tloc + 128]
                    achi = op("scalar", "ac", lambda e, b=bxn, hq=hq:
                              e.activation(out=hq, in_=pslot(b)[:, 0:128],
                                           func=AF.Copy), waits=[("pe", pexn)])
                    dvv = op("vector", "dv", lambda e, b=bxn, hq=hq, lq=lq:
                             e.scalar_tensor_tensor(out=lq, in0=pslot(b)[:, 0:128],
                                                    scalar=1.0, in1=hq, op0=ALU.mult,
                                                    op1=ALU.subtract),
                             waits=[("pe", pexn), ("ac", achi)])
                    bank_release(bxn, "ac", achi)
                    bank_release(bxn, "dv", dvv)

                    bba, wba = bank_acquire()
                    peba = op("tensor", "pe", lambda e, b=bba, dti=dti:
                              e.transpose(pslot(b)[:, 0:128],
                                          tmpB[:, 128 * dti:128 * dti + 128],
                                          identsb[:]),
                              waits=wba + [("dv", d5)])
                    h2 = bfXN2[:, dti * 1024 + 128 * tt:dti * 1024 + 128 * tt + 128]
                    l2 = bfXN2[:, 16384 + dti * 1024 + 128 * tt:
                               16384 + dti * 1024 + 128 * tt + 128]
                    wap = smalls[:, SM_RMS + dti:SM_RMS + dti + 1]
                    aca = op("scalar", "ac", lambda e, b=bba, h2=h2, wap=wap:
                             e.activation(out=h2, in_=pslot(b)[:, 0:128], func=AF.Copy,
                                          scale=wap), waits=[("pe", peba)])

                    def evb2(e, b=bba, h2=h2, l2=l2, wap=wap):
                        return e.scalar_tensor_tensor(out=l2, in0=pslot(b)[:, 0:128],
                                                      scalar=wap, in1=h2,
                                                      op0=ALU.mult, op1=ALU.subtract)
                    dvb = op("vector", "dv", evb2, waits=[("pe", peba), ("ac", aca)])
                    bank_release(bba, "ac", aca)
                    bank_release(bba, "dv", dvb)
                pe_after_tt[tt] = cnt["pe"]
            dvq = cnt["dv"]
            for hl in range(2):
                srcs = bfBF[:, 4096 * hl:4096 * hl + 4096].rearrange(
                    "p (d t) -> p d t", d=KT)
                dsts = xnsp[hl, :, :, 256 * quarter:256 * quarter + 256].rearrange(
                    "d p t -> p d t")
                spill_qg_prev = op("gpsimd", "qg", lambda e, srcs=srcs, dsts=dsts:
                                   e.dma_start(out=dsts, in_=srcs), k=16,
                                   waits=[("dv", dvq)] if hl == 0 else [])

        phaseA_dv = cnt["dv"]
        phaseA_ac = cnt["ac"]
        xnsp_done_qg = cnt["qg"]

        FZ_F = [0, 1024]
        FZ_Z = [2048, 3072]
        WSLOT = [4096, 8192]
        wslot_free = [[], []]
        slot_rr = [0]

        def load_convert_w(wname, j, slot, extra_wait=()):
            w32 = AR_TMP[:, 2048 * slot:2048 * slot + 2048]
            src = wT[wname][:, :, 128 * j:128 * j + 128]
            w = list(wslot_free[slot]) + list(extra_wait)
            wslot_free[slot] = []
            lq = op("sync", "q1", lambda e, w32=w32, src=src:
                    e.dma_start(out=w32.rearrange("p (k c) -> p k c", k=KT),
                                in_=src.rearrange("k p c -> p k c")),
                    k=16, waits=w)
            hi = bfBF[:, WSLOT[slot]:WSLOT[slot] + 2048]
            lo = bfBF[:, WSLOT[slot] + 2048:WSLOT[slot] + 4096]
            ah = op("scalar", "ac", lambda e, hi=hi, w32=w32:
                    e.activation(out=hi, in_=w32, func=AF.Copy), waits=[("q1", lq)])
            dl = op("vector", "dv", lambda e, lo=lo, w32=w32, hi=hi:
                    e.tensor_tensor(lo, w32, hi, ALU.subtract),
                    waits=[("q1", lq), ("ac", ah)])
            return hi, lo, ah, dl

        def mm_chain3(bank, hi, lo, xhi, xlo, th, waits):
            def chain(e):
                r = None
                for k in range(KT):
                    hk = hi[:, 128 * k:128 * k + 128]
                    lk = lo[:, 128 * k:128 * k + 128]
                    xh = xhi[:, 1024 * k + 512 * th:1024 * k + 512 * th + 512]
                    xl = xlo[:, 1024 * k + 512 * th:1024 * k + 512 * th + 512]
                    e.matmul(pslot(bank), hk, xh, start=(k == 0), stop=False)
                    e.matmul(pslot(bank), hk, xl, start=False, stop=False)
                    r = e.matmul(pslot(bank), lk, xh, start=False, stop=(k == KT - 1))
                return r
            return op("tensor", "pe", chain, waits=waits)

        phase1_ready = [("dv", phaseA_dv), ("ac", phaseA_ac), ("qg", xnsp_done_qg)]
        scan_done_dv = 0
        zgs_guard = [0, 0]
        for j in range(KT):
            fb = bfBF[:, FZ_F[j % 2]:FZ_F[j % 2] + 1024]
            zb = bfBF[:, FZ_Z[j % 2]:FZ_Z[j % 2] + 1024]
            fz_guard = [("dv", scan_done_dv)] if j >= 2 else []
            per_w = {}
            for wn in ["wzg", "wz", "wf"]:
                slot = slot_rr[0] % 2
                slot_rr[0] += 1
                extra = phase1_ready if j == 0 else []
                hi, lo, ah, dl = load_convert_w(wn, j, slot, extra_wait=extra)
                pe_chain = []
                for th in range(2):
                    b, bw = bank_acquire()
                    pv = mm_chain3(b, hi, lo, xn2hi, xn2lo, th,
                                   waits=bw + [("ac", ah), ("dv", dl)] +
                                   (phase1_ready if (j == 0 and wn == "wzg" and th == 0)
                                    else []))
                    pe_chain.append((b, pv))
                wslot_free[slot] = [("pe", pe_chain[-1][1])]
                per_w[wn] = pe_chain
            for th in range(2):
                bzg, pzg = per_w["wzg"][th]
                bz, pz = per_w["wz"][th]
                bf_, pf = per_w["wf"][th]
                zgs = AR_TMP[:, 4096 + 512 * th:4096 + 512 * th + 512]
                a_zg = op("scalar", "ac", lambda e, b=bzg, zgs=zgs:
                          e.activation(out=zgs, in_=pslot(b), func=AF.Sigmoid),
                          waits=[("pe", pzg), ("dv", zgs_guard[th])])
                bank_release(bzg, "ac", a_zg)
                d_z = op("vector", "dv", lambda e, b=bz, zgs=zgs, zb=zb, th=th:
                         e.tensor_tensor(zb[:, 512 * th:512 * th + 512], pslot(b), zgs,
                                         ALU.mult),
                         waits=[("pe", pz), ("ac", a_zg)])
                bank_release(bz, "dv", d_z)
                zgs_guard[th] = d_z
                drip_gather(4)
                fbias = smalls[:, SM_WFB + j:SM_WFB + j + 1]
                a_f = op("scalar", "ac", lambda e, b=bf_, fb=fb, th=th, fbias=fbias:
                         e.activation(out=fb[:, 512 * th:512 * th + 512], in_=pslot(b),
                                      func=AF.Sigmoid, bias=fbias),
                         waits=[("pe", pf)] + (fz_guard if th == 0 else []))
                bank_release(bf_, "ac", a_f)
            hj = bfHALL[:, 1024 * j:1024 * j + 1024]
            pj = bfHALL[:, 16384 + 1024 * j:16384 + 1024 * j + 1024]
            def scans(e, fb=fb, zb=zb, hj=hj, pj=pj, j=j):
                e.tensor_tensor_scan(hj, fb, zb, 0.0, ALU.mult, ALU.add)
                return e.tensor_tensor_scan(pj, fb, zerob[:], 1.0, ALU.mult, ALU.add)
            scan_done_dv = op("vector", "dv", scans, waits=[("ac", cnt["ac"])])
            drip_gather(6)

        all_scans_dv = scan_done_dv

        bsrc = bfHALL[:, 0:16384].rearrange(
            "p (j t) -> p j t", j=KT)[:, :, 1023:1024].squeeze(2)
        all_scans_dv = op("vector", "dv", lambda e, bsrc=bsrc:
                          e.tensor_copy(bndsb[:, 0:16], bsrc))
        bq = op("gpsimd", "qg", lambda e:
                e.dma_start(out=bnd_in, in_=bndsb[:, 0:16]),
                k=16, waits=[("dv", all_scans_dv)])
        cc1 = op("gpsimd", "cc", lambda e: e.collective_compute(
            "AllGather", ALU.bypass,
            replica_groups=[[0, 1], [2, 3], [4, 5], [6, 7]],
            ins=[bnd_in], outs=[bnd_out]), waits=[("qg", bq)])
        hbq = op("gpsimd", "qg", lambda e: e.dma_start(out=bndsb[:, 64:80],
                                                       in_=bnd_out[0]),
                 k=16, waits=[("cc", cc1)])

        hbm_ac = op("scalar", "ac", lambda e:
                    e.activation(out=bndsb[:, 128:144], in_=bndsb[:, 64:80],
                                 func=AF.Copy,
                                 scale=smalls[:, SM_MSK:SM_MSK + 1]),
                    waits=[("qg", hbq)])
        hbm_dv = op("vector", "dv", lambda e: e.memset(warm[:], 0.0),
                    waits=[("ac", hbm_ac)])

        def corrections(e):
            r = None
            for j in range(KT):
                hj = bfHALL[:, 1024 * j:1024 * j + 1024]
                pj = bfHALL[:, 16384 + 1024 * j:16384 + 1024 * j + 1024]
                r = e.scalar_tensor_tensor(out=hj, in0=pj,
                                           scalar=bndsb[:, 128 + j:129 + j],
                                           in1=hj, op0=ALU.mult, op1=ALU.add)
            return r
        corr_dv = op("vector", "dv", corrections, waits=[("dv", hbm_dv)])

        gas_guard = list(zgs_guard)
        for j in range(KT):
            per_w = {}
            for wn in ["wga", "wgb"]:
                slot = slot_rr[0] % 2
                slot_rr[0] += 1
                hi, lo, ah, dl = load_convert_w(wn, j, slot)
                pe_chain = []
                for th in range(2):
                    b, bw = bank_acquire()
                    pv = mm_chain3(b, hi, lo, xn2hi, xn2lo, th,
                                   waits=bw + [("ac", ah), ("dv", dl)])
                    pe_chain.append((b, pv))
                wslot_free[slot] = [("pe", pe_chain[-1][1])]
                per_w[wn] = pe_chain
            ogb = bfBF[:, FZ_F[j % 2]:FZ_F[j % 2] + 1024]
            for th in range(2):
                bga, pga = per_w["wga"][th]
                bgb, pgb = per_w["wgb"][th]
                gas = AR_TMP[:, 4096 + 512 * th:4096 + 512 * th + 512]
                a_ga = op("scalar", "ac", lambda e, b=bga, gas=gas:
                          e.activation(out=gas, in_=pslot(b), func=AF.Silu),
                          waits=[("pe", pga), ("dv", gas_guard[th])])
                bank_release(bga, "ac", a_ga)
                d_og = op("vector", "dv", lambda e, b=bgb, gas=gas, ogb=ogb, th=th:
                          e.tensor_tensor(ogb[:, 512 * th:512 * th + 512], pslot(b),
                                          gas, ALU.mult),
                          waits=[("pe", pgb), ("ac", a_ga)])
                bank_release(bgb, "dv", d_og)
                gas_guard[th] = d_og
                drip_gather(8)
            if j >= 1:
                jp = j - 1
                gp = bfHALL[:, 16384 + 1024 * jp:16384 + 1024 * jp + 1024]
                hp = bfHALL[:, 1024 * jp:1024 * jp + 1024]
                ogp = bfBF[:, FZ_F[jp % 2]:FZ_F[jp % 2] + 1024]
                op("vector", "dv", lambda e, gp=gp, hp=hp, ogp=ogp:
                   e.tensor_tensor(gp, ogp, hp, ALU.mult), waits=[("dv", corr_dv)])
        op("vector", "dv", lambda e: e.memset(warm[:], 0.0))
        gp = bfHALL[:, 16384 + 1024 * 15:16384 + 1024 * 15 + 1024]
        hp = bfHALL[:, 1024 * 15:1024 * 15 + 1024]
        ogp = bfBF[:, FZ_F[15 % 2]:FZ_F[15 % 2] + 1024]
        op("vector", "dv", lambda e, gp=gp, hp=hp, ogp=ogp:
           e.tensor_tensor(gp, ogp, hp, ALU.mult))
        all_gated_dv = cnt["dv"]
        drip_gather(len(gchunks) * 20)

        pe_before_wout = cnt["pe"]
        guslot_free = [[], []]
        gu_loads = {}

        def emit_gu_load(i, arr, extra=()):
            slot = (2 * i + arr) % 2
            lw = list(guslot_free[slot]) + list(extra)
            guslot_free[slot] = []
            dstW = bfXNT[:, 4096 * slot:4096 * slot + 4096]
            dW5 = dstW.rearrange("p (r ti hl c) -> p r ti hl c", r=8, ti=2, hl=2)
            lq = None
            for ti in range(2):
                for hl in range(2):
                    srcW = agg[arr, :, i + IT * ti, hl].rearrange("r p c -> p r c")
                    dstWt = dW5[:, :, ti, hl]
                    lq = op("gpsimd", "qg",
                            lambda e, dstWt=dstWt, srcW=srcW:
                            e.dma_start(out=dstWt, in_=srcW),
                            k=16,
                            waits=(lw + [("cc", cc_arr[arr]),
                                         ("cc", cc_arr[2])])
                            if (ti == 0 and hl == 0) else [])
            gu_loads[(i, arr)] = (slot, lq)

        gated = bfHALL[:, 16384:32768]
        emit_gu_load(0, 0)
        emit_gu_load(0, 1)
        ysp_guard = [0, 0]
        for j in range(KT):
            slot = slot_rr[0] % 2
            slot_rr[0] += 1
            hi, lo, ah, dl = load_convert_w("wo", j, slot)
            for th in range(2):
                b, bw = bank_acquire()

                def chain2(e, b=b, hi=hi, lo=lo, th=th):
                    r = None
                    for k in range(KT):
                        hk = hi[:, 128 * k:128 * k + 128]
                        lk = lo[:, 128 * k:128 * k + 128]
                        g = gated[:, 1024 * k + 512 * th:1024 * k + 512 * th + 512]
                        e.matmul(pslot(b), hk, g, start=(k == 0), stop=False)
                        r = e.matmul(pslot(b), lk, g, start=False, stop=(k == KT - 1))
                    return r
                pv = op("tensor", "pe", chain2,
                        waits=bw + [("ac", ah), ("dv", dl)] +
                        ([("dv", all_gated_dv)] if (j == 0 and th == 0) else []))
                if th == 1:
                    wslot_free[slot] = [("pe", pv)]
                ybuf = AR_TMP[:, 5120 + 512 * th:5120 + 512 * th + 512]
                bias = smalls[:, SM_WOB + j:SM_WOB + j + 1]
                av = op("vector", "dv", lambda e, b=b, ybuf=ybuf, bias=bias:
                        e.tensor_scalar(out=ybuf, in0=pslot(b), scalar1=bias,
                                        scalar2=None, op0=ALU.add),
                        waits=[("pe", pv), ("qg", ysp_guard[th])])
                bank_release(b, "dv", av)
                ysp_guard[th] = op("gpsimd", "qg", lambda e, ybuf=ybuf, j=j, th=th:
                                   e.dma_start(out=ysp[j, :, 512 * th:512 * th + 512],
                                               in_=ybuf),
                                   k=16, waits=[("dv", av)])
        ysp_qg = cnt["qg"]
        wout_pe = cnt["pe"]

        xn_rel = 0
        for hl in range(2):
            for dti in range(KT):
                if hl == 0:
                    dst = bfHALL[:, 1024 * dti:1024 * dti + 1024]
                    wlist = [("dv", all_gated_dv)] if dti == 0 else []
                else:
                    dst = bfXN2[:, 1024 * dti:1024 * dti + 1024]
                    wlist = [("pe", pe_before_wout)] if dti == 0 else []
                xn_rel = op("sync", "q1", lambda e, dst=dst, hl=hl, dti=dti:
                            e.dma_start(out=dst, in_=xnsp[hl, dti]), k=16, waits=wlist)
        xnhi_sb = bfHALL[:, 0:16384]
        xnlo_sb = bfXN2[:, 0:16384]

        def ycol(j, th):
            if j < 8:
                return AR_XN2[:, 8192 + 1024 * j + 512 * th:
                              8192 + 1024 * j + 512 * th + 512]
            return AR_HALL[:, 8192 + 1024 * (j - 8) + 512 * th:
                           8192 + 1024 * (j - 8) + 512 * th + 512]

        def yfull(j):
            if j < 8:
                return AR_XN2[:, 8192 + 1024 * j:8192 + 1024 * j + 1024]
            return AR_HALL[:, 8192 + 1024 * (j - 8):8192 + 1024 * (j - 8) + 1024]

        dn_free = [[], [], []]
        chunks = []
        i0 = 0
        while i0 < IT:
            chunks.append((i0, min(DCHUNK, IT - i0)))
            i0 += DCHUNK

        first_gu = [True]
        h_guard = {}
        gas_guard2 = list(gas_guard)
        htmp_guard = [dict(th=0), None]
        htmp_first = [True, True]
        dve_after_chunk = 0
        for ci, (c0, cn) in enumerate(chunks):
            gate_pair = up_pair = None
            h_ready = []
            for ii in range(cn):
                i = c0 + ii
                pair = {}
                for arr in (0, 1):
                    if (i, arr) not in gu_loads:
                        emit_gu_load(i, arr)
                    slot, lq = gu_loads.pop((i, arr))
                    dstW = bfXNT[:, 4096 * slot:4096 * slot + 4096]
                    wsl = dstW
                    pe_pair = []
                    for th in range(2):
                        b, bw = bank_acquire()

                        def chaing(e, b=b, wsl=wsl, th=th):
                            r = None
                            for k in range(KT):
                                hk = wsl[:, 256 * k:256 * k + 128]
                                lk = wsl[:, 256 * k + 128:256 * k + 256]
                                xh = xnhi_sb[:, 1024 * k + 512 * th:
                                             1024 * k + 512 * th + 512]
                                xl = xnlo_sb[:, 1024 * k + 512 * th:
                                             1024 * k + 512 * th + 512]
                                e.matmul(pslot(b), hk, xh, start=(k == 0), stop=False)
                                e.matmul(pslot(b), hk, xl, start=False, stop=False)
                                r = e.matmul(pslot(b), lk, xh, start=False,
                                             stop=(k == KT - 1))
                            return r
                        pv = op("tensor", "pe", chaing,
                                waits=bw + [("qg", lq)] +
                                ([("q1", xn_rel)] if (ci == 0 and ii == 0 and th == 0
                                                      and arr == 0) else []))
                        pe_pair.append((b, pv))
                    guslot_free[slot] = [("pe", pe_pair[1][1])]
                    pair[arr] = pe_pair
                gate_pair, up_pair = pair[0], pair[1]
                for th in range(2):
                    bg, pg = gate_pair[th]
                    bu, pu = up_pair[th]
                    gas = AR_TMP[:, 4096 + 512 * th:4096 + 512 * th + 512]
                    ag = op("scalar", "ac", lambda e, b=bg, gas=gas:
                            e.activation(out=gas, in_=pslot(b), func=AF.Silu),
                            waits=[("pe", pg), ("dv", gas_guard2[th])])
                    bank_release(bg, "ac", ag)
                    htmp = AR_TMP[:, 5120 + 512 * th:5120 + 512 * th + 512]
                    hhi = bfBF[:, 4096 + 2048 * ii + 512 * th:
                               4096 + 2048 * ii + 512 * th + 512]
                    hlo = bfBF[:, 4096 + 2048 * ii + 1024 + 512 * th:
                               4096 + 2048 * ii + 1024 + 512 * th + 512]
                    hw = [("pe", pu), ("ac", ag)]
                    if (ii, th) in h_guard:
                        hw.append(("pe", h_guard[(ii, th)]))
                    if htmp_first[th]:
                        hw.append(("qg", ysp_guard[th]))
                        htmp_first[th] = False

                    dm = op("vector", "dv", lambda e, b=bu, gas=gas, htmp=htmp:
                            e.tensor_tensor(htmp, pslot(b), gas, ALU.mult), waits=hw)
                    ah2 = op("scalar", "ac", lambda e, htmp=htmp, hhi=hhi:
                             e.activation(out=hhi, in_=htmp, func=AF.Copy),
                             waits=[("dv", dm)])
                    dh = op("vector", "dv", lambda e, htmp=htmp, hhi=hhi, hlo=hlo:
                            e.scalar_tensor_tensor(out=hlo, in0=htmp, scalar=1.0,
                                                   in1=hhi, op0=ALU.mult,
                                                   op1=ALU.subtract),
                            waits=[("ac", ah2)])
                    bank_release(bu, "dv", dm)
                    gas_guard2[th] = dh
                    h_ready.append(dh)
            h_all_ready = max(h_ready)
            for j in range(KT):
                dslot = j % 3
                dw = list(dn_free[dslot])
                dn_free[dslot] = []
                dsb = bfBF[:, 1024 * dslot:1024 * dslot + 256 * cn]
                dsb4 = dsb.rearrange("p (i hl c) -> p i hl c", i=cn, hl=2)
                for hl in range(2):
                    srcD = agg[2, j // 2, 2 * c0 + (j % 2):2 * (c0 + cn):2, hl]
                    dwx = (dw + ([("cc", cc_arr[2]), ("dv", all_gated_dv)]
                                 if ci == 0 else [])) if hl == 0 else []
                    lq = op("gpsimd", "qg",
                            lambda e, dsb4=dsb4, srcD=srcD, hl=hl:
                            e.dma_start(out=dsb4[:, :, hl],
                                        in_=srcD.rearrange("i p c -> p i c")),
                            k=16, waits=dwx)
                last_pv = 0
                for th in range(2):
                    b, bw = bank_acquire()

                    def chaind(e, b=b, dsb=dsb, th=th, cn=cn):
                        r = None
                        for ii in range(cn):
                            hk = dsb[:, 256 * ii:256 * ii + 128]
                            lk = dsb[:, 256 * ii + 128:256 * ii + 256]
                            xh = bfBF[:, 4096 + 2048 * ii + 512 * th:
                                      4096 + 2048 * ii + 512 * th + 512]
                            xl = bfBF[:, 4096 + 2048 * ii + 1024 + 512 * th:
                                      4096 + 2048 * ii + 1024 + 512 * th + 512]
                            e.matmul(pslot(b), hk, xh, start=(ii == 0), stop=False)
                            e.matmul(pslot(b), hk, xl, start=False, stop=False)
                            r = e.matmul(pslot(b), lk, xh, start=False,
                                         stop=(ii == cn - 1))
                        return r
                    pv = op("tensor", "pe", chaind,
                            waits=bw + [("qg", lq), ("dv", h_all_ready)])
                    last_pv = pv
                    yc = ycol(j, th)
                    if ci == 0:
                        dy = op("vector", "dv", lambda e, b=b, yc=yc:
                                e.tensor_copy(yc, pslot(b)), waits=[("pe", pv)])
                    else:
                        dy = op("vector", "dv", lambda e, b=b, yc=yc:
                                e.scalar_tensor_tensor(out=yc, in0=pslot(b),
                                                       scalar=1.0, in1=yc,
                                                       op0=ALU.mult, op1=ALU.add),
                                waits=[("pe", pv)])
                    bank_release(b, "dv", dy)
                dn_free[dslot] = [("pe", last_pv)]
            for ii in range(DCHUNK):
                for th in range(2):
                    h_guard[(ii, th)] = cnt["pe"]
            dve_after_chunk = cnt["dv"]
        ffn_dv = cnt["dv"]

        g_buf = AR_TMP[:, 0:1024]
        r_done = {}
        prev_r = ffn_dv
        for j in range(KT):
            ya = AR_TMP[:, 1024 + 1024 * (j % 2):1024 + 1024 * (j % 2) + 1024]
            w = [("dv", r_done[j - 2])] if j >= 2 else []
            yq = op("sync", "q1", lambda e, ya=ya, j=j:
                    e.dma_start(out=ya, in_=ysp[j]), k=16,
                    waits=w + ([("qg", ysp_qg)] if j == 0 else []))
            aj = op("scalar", "ac", lambda e, j=j:
                    e.activation(out=g_buf, in_=yfull(j), func=AF.Silu),
                    waits=[("dv", ffn_dv if j == 0 else prev_r)])

            def rops(e, ya=ya, j=j):
                yj = yfull(j)
                e.scalar_tensor_tensor(out=ya, in0=xnhi_sb[:, 1024 * j:1024 * j + 1024],
                                       scalar=1.0, in1=ya, op0=ALU.mult, op1=ALU.add)
                e.scalar_tensor_tensor(out=ya, in0=xnlo_sb[:, 1024 * j:1024 * j + 1024],
                                       scalar=1.0, in1=ya, op0=ALU.mult, op1=ALU.add)
                return e.scalar_tensor_tensor(out=yj, in0=g_buf, scalar=1.0, in1=ya,
                                              op0=ALU.mult, op1=ALU.mult)
            prev_r = op("vector", "dv", rops,
                        waits=[("q1", yq), ("ac", aj), ("dv", ffn_dv)])
            r_done[j] = prev_r
        all_r_dv = prev_r

        out_q_map = {}
        prev_out_dv = all_r_dv
        for tt in range(8):
            xb = AR_TMP[:, 3072:5120]
            w = [("dv", prev_out_dv)] if tt >= 1 else [("dv", all_r_dv)]
            xq = op("sync", "q1", lambda e, xb=xb, tt=tt:
                    e.dma_start(out=xb, in_=x_in[tt]), k=16, waits=w)
            outsb = AR_XN2[:, 2048 * (tt % 2):2048 * (tt % 2) + 2048]
            grp_dv = 0
            for g4 in range(4):
                b, bw = bank_acquire()

                def tchain(e, b=b, tt=tt, g4=g4):
                    r = None
                    for jj in range(4):
                        j = 4 * g4 + jj
                        r = e.transpose(
                            pslot(b)[:, 128 * jj:128 * jj + 128],
                            yfull(j)[:, 128 * tt:128 * tt + 128],
                            identsb[:])
                    return r
                pv = op("tensor", "pe", tchain, waits=bw + [("dv", all_r_dv)])
                wv = [("pe", pv), ("q1", xq)]
                if tt >= 2 and g4 == 0:
                    wv.append(("qg", out_q_map[tt - 2]))
                dv = op("vector", "dv", lambda e, b=b, outsb=outsb, xb=xb, g4=g4:
                        e.scalar_tensor_tensor(out=outsb[:, 512 * g4:512 * g4 + 512],
                                               in0=pslot(b), scalar=1.0,
                                               in1=xb[:, 512 * g4:512 * g4 + 512],
                                               op0=ALU.mult, op1=ALU.add),
                        waits=wv)
                bank_release(b, "dv", dv)
                grp_dv = dv
            prev_out_dv = grp_dv
            out_q_map[tt] = op("gpsimd", "qg", lambda e, outsb=outsb, tt=tt:
                               e.dma_start(out=out_dram[tt], in_=outsb),
                               k=16, waits=[("dv", grp_dv)])
        wait("gpsimd", "qg", cnt["qg"])
        wait("sync", "q1", cnt["q1"])

        with nc.Block() as block:
            @block.sync
            def _(e):
                for f in prog["sync"]:
                    f(e)

            @block.gpsimd
            def _(e):
                for f in prog["gpsimd"]:
                    f(e)

            @block.vector
            def _(e):
                for f in prog["vector"]:
                    f(e)

            @block.scalar
            def _(e):
                for f in prog["scalar"]:
                    f(e)

            @block.tensor
            def _(e):
                for f in prog["tensor"]:
                    f(e)
    return nc


def kernel(L0_attn_x, L0_attn_rms_w, L0_attn_Wf_w, L0_attn_Wf_b, L0_attn_Wzg_w,
           L0_attn_Wz_w, L0_attn_Wga_w, L0_attn_Wgb_w, L0_attn_Wout_w, L0_attn_Wout_b,
           L0_ffn_lut, L0_ffn_gate_idx, L0_ffn_up_idx, L0_ffn_down_idx):
    x = np.ascontiguousarray(np.asarray(L0_attn_x, dtype=np.float32))
    B, Tfull, Dm = x.shape
    xs = x.reshape(B, 2, T, Dm)

    def prepT(w):
        return np.ascontiguousarray(np.asarray(w, np.float32).T).reshape(KT, 128, D)

    wfT = prepT(L0_attn_Wf_w)
    wzgT = prepT(L0_attn_Wzg_w)
    wzT = prepT(L0_attn_Wz_w)
    wgaT = prepT(L0_attn_Wga_w)
    wgbT = prepT(L0_attn_Wgb_w)
    woT = prepT(L0_attn_Wout_w)

    rms_w = np.asarray(L0_attn_rms_w, np.float32)
    wf_b = np.asarray(L0_attn_Wf_b, np.float32)
    wout_b = np.asarray(L0_attn_Wout_b, np.float32)
    lut = np.asarray(L0_ffn_lut, np.float32)
    gidxT = np.ascontiguousarray(np.asarray(L0_ffn_gate_idx, np.int32).T)
    uidxT = np.ascontiguousarray(np.asarray(L0_ffn_up_idx, np.int32).T)
    didxT = np.ascontiguousarray(np.asarray(L0_ffn_down_idx, np.int32).T)

    ident = np.eye(128, dtype=np.float32)
    lut128 = np.tile(lut.reshape(1, 16), (128, 1)).astype(np.float32)

    nc = build_nc()
    in_maps = []
    for c in range(8):
        b, half = c // 2, c % 2
        smalls = np.zeros((128, 128), np.float32)
        smalls[:, 0:16] = rms_w.reshape(16, 128).T
        smalls[:, 16:32] = wf_b.reshape(16, 128).T
        smalls[:, 32:48] = wout_b.reshape(16, 128).T
        smalls[:, 48] = 1.0 if half == 1 else 0.0
        smalls[:, 64:80] = EPS
        smalls[:, 80:96] = 1.0 / D
        gs = np.ascontiguousarray(gidxT[256 * c:256 * c + 256]).reshape(2, 128, INTER)
        us = np.ascontiguousarray(uidxT[256 * c:256 * c + 256]).reshape(2, 128, INTER)
        ds = np.ascontiguousarray(didxT[:, 256 * c:256 * c + 256]).reshape(IT, 128, 256)
        in_maps.append({
            "x_in": np.ascontiguousarray(xs[b, half]).reshape(8, 128, D),
            "wfT": wfT, "wzgT": wzgT, "wzT": wzT, "wgaT": wgaT, "wgbT": wgbT,
            "woT": woT,
            "smalls": smalls, "lut128": lut128, "ident": ident,
            "gidx": gs, "uidx": us, "didx": ds,
        })
    res = run_bass_kernel_spmd(nc, in_maps, core_ids=list(range(8)))
    out = np.empty((B, Tfull, Dm), np.float32)
    for c in range(8):
        b, half = c // 2, c % 2
        out[b, T * half:T * half + T] = res.results[c]["out"].reshape(T, Dm)
    return out


# version 13
# speedup vs baseline: 1.1326x; 1.1326x over previous
import numpy as np
from contextlib import ExitStack

import concourse.bass as bass
import concourse.mybir as mybir
from concourse.bass_utils import run_bass_kernel_spmd

F32 = mybir.dt.float32
BF16 = mybir.dt.bfloat16
I32 = mybir.dt.int32
AF = mybir.ActivationFunctionType
ALU = mybir.AluOpType
AXX = mybir.AxisListType.X

D = 2048
S = 2048
INTER = 5504
T = 1024
KT = 16
IT = 43
EPS = 1e-5
DCHUNK = 4

XN2_N = 16384
HALL_N = 16384
BF_N = 6144
XNT_N = 4096
TMP_N = 6144
IDXN = 2048


def build_nc():
    es = ExitStack()
    nc = bass.Bass("TRN2", target_bir_lowering=False)

    x_in = nc.dram_tensor("x_in", [8, 128, D], F32, kind="ExternalInput").ap()
    wT = {}
    for nm in ["wf", "wzg", "wz", "wga", "wgb", "wo"]:
        wT[nm] = nc.dram_tensor(nm + "T", [KT, 128, D], F32, kind="ExternalInput").ap()
    smalls_in = nc.dram_tensor("smalls", [128, 128], F32, kind="ExternalInput").ap()
    lut_in = nc.dram_tensor("lut128", [128, 16], F32, kind="ExternalInput").ap()
    ident_in = nc.dram_tensor("ident", [128, 128], F32, kind="ExternalInput").ap()
    gidx_in = nc.dram_tensor("gidx", [2, 128, INTER], I32, kind="ExternalInput").ap()
    uidx_in = nc.dram_tensor("uidx", [2, 128, INTER], I32, kind="ExternalInput").ap()
    didx_in = nc.dram_tensor("didx", [IT, 128, 256], I32, kind="ExternalInput").ap()
    out_dram = nc.dram_tensor("out", [8, 128, D], F32, kind="ExternalOutput").ap()

    xnsp = nc.dram_tensor("xnsp", [2, KT, 128, T], BF16, kind="Internal").ap()
    ysp = nc.dram_tensor("ysp", [KT, 128, T], F32, kind="Internal").ap()
    stg = nc.dram_tensor("stg", [3, 86, 2, 128, 128], BF16, kind="Internal").ap()
    agg = nc.dram_tensor("agg", [3, 8, 86, 2, 128, 128], BF16, kind="Internal",
                     addr_space="Shared").ap()
    bnd_in = nc.dram_tensor("bnd_in", [128, 16], BF16, kind="Internal").ap()
    bnd_out = nc.dram_tensor("bnd_out", [2, 128, 16], BF16, kind="Internal").ap()

    with es:
        AR_XN2 = es.enter_context(nc.sbuf_tensor("AR_XN2", [128, XN2_N], F32))
        AR_HALL = es.enter_context(nc.sbuf_tensor("AR_HALL", [128, HALL_N], F32))
        AR_BF = es.enter_context(nc.sbuf_tensor("AR_BF", [128, BF_N], F32))
        AR_XNT = es.enter_context(nc.sbuf_tensor("AR_XNT", [128, XNT_N], F32))
        AR_TMP = es.enter_context(nc.sbuf_tensor("AR_TMP", [128, TMP_N], F32))
        AR_IDX = es.enter_context(nc.sbuf_tensor("AR_IDX", [128, IDXN], I32))
        smalls = es.enter_context(nc.sbuf_tensor("smalls_sb", [128, 128], F32))
        lutb = es.enter_context(nc.sbuf_tensor("lutb", [128, 16], F32))
        luthi = es.enter_context(nc.sbuf_tensor("luthi", [128, 16], F32))
        lutlo = es.enter_context(nc.sbuf_tensor("lutlo", [128, 16], F32))
        identsb = es.enter_context(nc.sbuf_tensor("identsb", [128, 128], F32))
        stats = es.enter_context(nc.sbuf_tensor("stats", [128, 256], F32))
        zerob = es.enter_context(nc.sbuf_tensor("zerob", [128, 1024], BF16))
        bndsb = es.enter_context(nc.sbuf_tensor("bndsb", [128, 256], BF16))
        warm = es.enter_context(nc.sbuf_tensor("warm", [128, 256], F32))
        psum = es.enter_context(nc.psum_tensor("psum", [128, 4096], F32))

        sem_q1 = es.enter_context(nc.semaphore("q1"))
        sem_qg = es.enter_context(nc.semaphore("qg"))
        sem_pe = es.enter_context(nc.semaphore("pe"))
        sem_dv = es.enter_context(nc.semaphore("dv"))
        sem_ac = es.enter_context(nc.semaphore("ac"))
        sem_cc = es.enter_context(nc.semaphore("cc"))

        bfBF = AR_BF[:].bitcast(BF16)
        bfXNT = AR_XNT[:].bitcast(BF16)
        bfXN2 = AR_XN2[:].bitcast(BF16)
        bfHALL = AR_HALL[:].bitcast(BF16)

        prog = {k: [] for k in ["sync", "gpsimd", "vector", "scalar", "tensor"]}
        cnt = {"q1": 0, "qg": 0, "pe": 0, "dv": 0, "ac": 0, "cc": 0}
        sems = {"q1": sem_q1, "qg": sem_qg, "pe": sem_pe, "dv": sem_dv,
                "ac": sem_ac, "cc": sem_cc}

        def emit(eng, fn):
            prog[eng].append(fn)

        def op(eng, semname, emit_fn, k=1, waits=()):
            for wsem, wval in waits:
                wh, wv = sems[wsem], int(wval)
                if wv > 0:
                    emit(eng, (lambda e, wh=wh, wv=wv: e.wait_ge(wh, wv)))
            cnt[semname] += k
            sh = sems[semname]
            emit(eng, (lambda e, fn=emit_fn, sh=sh, k=k: fn(e).then_inc(sh, k)))
            return cnt[semname]

        def wait(eng, semname, value):
            wh, wv = sems[semname], int(value)
            emit(eng, (lambda e, wh=wh, wv=wv: e.wait_ge(wh, wv)))

        bank_state = [[] for _ in range(8)]
        bank_rr = [0]

        def bank_acquire():
            b = bank_rr[0] % 8
            bank_rr[0] += 1
            w = bank_state[b]
            bank_state[b] = []
            return b, list(w)

        def bank_release(b, semname, value):
            bank_state[b].append((semname, value))

        def pslot(b):
            return psum[:, 512 * b:512 * (b + 1)]

        SM_RMS, SM_WFB, SM_WOB, SM_MSK, SM_EPS, SM_INVD = 0, 16, 32, 48, 64, 80

        op("sync", "q1", lambda e: e.dma_start(out=smalls[:], in_=smalls_in), k=16)
        op("sync", "q1", lambda e: e.dma_start(out=lutb[:, 0:16], in_=lut_in), k=16)
        op("sync", "q1", lambda e: e.dma_start(out=identsb[:], in_=ident_in), k=16)
        q1_init = cnt["q1"]

        def dve_warm(e):
            wb = warm[:].bitcast(BF16)
            e.memset(warm[:], 0.0)
            e.memset(wb[:, 0:256], 0.0)
            for w0, w1 in [(warm[:, 0:128], warm[:, 128:256]),
                           (wb[:, 0:128], wb[:, 128:256])]:
                e.tensor_scalar(out=w0, in0=w0, scalar1=1.0, scalar2=None,
                                op0=ALU.is_equal)
                e.scalar_tensor_tensor(out=w0, in0=w0, scalar=1.0, in1=w0,
                                       op0=ALU.mult, op1=ALU.add)
                e.tensor_copy(w1, w0)
                e.tensor_tensor(w0, w0, w1, ALU.subtract)
            e.tensor_copy(wb[:, 256:384], warm[:, 64:128].bitcast(BF16))
            e.tensor_copy(warm[:, 64:128], wb[:, 0:64])
            e.tensor_tensor(wb[:, 0:128], warm[:, 0:64].bitcast(BF16),
                            wb[:, 256:384], ALU.subtract)
            e.tensor_tensor_scan(wb[:, 0:64], wb[:, 64:128], wb[:, 128:192], 0.0,
                                 ALU.mult, ALU.add)
            e.tensor_reduce(warm[:, 128:129], warm[:, 0:64], AXX, ALU.add)
            e.reciprocal(warm[:, 130:131], warm[:, 129:130])
            e.memset(zerob[:], 0.0)
            return e.memset(warm[:], 0.0)
        dv_warm = op("vector", "dv", dve_warm)

        wait("vector", "q1", 32)

        def lut_prep(e):
            lbh = warm[:, 64:96].bitcast(BF16)
            e.tensor_copy(lbh[:, 0:16], lutb[:, 0:16])
            e.memset(warm[:, 0:64], 0.0)
            e.memset(warm[:, 0:64], 0.0)
            e.tensor_copy(luthi[:, 0:16], lbh[:, 0:16])
            e.memset(warm[:, 0:64], 0.0)
            e.memset(warm[:, 0:64], 0.0)
            llh = warm[:, 96:128].bitcast(BF16)
            e.tensor_tensor(llh[:, 0:16], lutb[:, 0:16], luthi[:, 0:16], ALU.subtract)
            e.memset(warm[:, 0:64], 0.0)
            e.memset(warm[:, 0:64], 0.0)
            return e.tensor_copy(lutlo[:, 0:16], llh[:, 0:16])
        dv_lut = op("vector", "dv", lut_prep)


        def act_warm(e):
            e.activation(out=warm[:, 0:128], in_=warm[:, 128:256], func=AF.Square)
            return e.activation(out=warm[:, 0:128], in_=warm[:, 128:256], func=AF.Sqrt)
        op("scalar", "ac", act_warm, waits=[("dv", dv_warm)])

        gchunks = []
        for arr, src in [(0, gidx_in), (1, uidx_in)]:
            for ti in range(2):
                j = 0
                while j < IT:
                    n = min(8, IT - j)
                    gchunks.append(("gu", arr, src, ti, j, n))
                    j += n
        j = 0
        while j < IT:
            n = min(4, IT - j)
            gchunks.append(("dn", 2, didx_in, 0, j, n))
            j += n
        GSLOT = [0, 4096]
        cc_arr = {}
        f32XNT = AR_XNT

        class Gather:
            def __init__(self):
                self.gi = 0
                self.k = -1
                self.loaded = 0
                self.done = False

            def _load(self, gi):
                kind, arr, srcA, ti, j0, n = gchunks[gi]
                ncols = n * 128 if kind == "gu" else n * 256
                slot = gi % 2
                idxs = AR_IDX[:, 1024 * slot:1024 * slot + ncols]
                w = []
                if gi >= 2:
                    w.append(("dv", chunk_dv[gi - 2]))

                if kind == "gu":
                    sap = srcA[ti, :, 128 * j0:128 * j0 + ncols]
                    op("gpsimd", "qg", lambda e, idxs=idxs, sap=sap:
                       e.dma_start(out=idxs, in_=sap), k=16, waits=w)
                else:
                    sap = srcA[j0:j0 + n].rearrange("i p q -> p i q")
                    dst3 = idxs.rearrange("p (i q) -> p i q", i=n)
                    op("gpsimd", "qg", lambda e, dst3=dst3, sap=sap:
                       e.dma_start(out=dst3, in_=sap), k=16, waits=w)
                chunk_ldq[gi] = cnt["qg"]
                self.loaded = gi + 1

            def step(self):
                if self.done:
                    return False
                gi = self.gi
                if gi >= len(gchunks):
                    self.done = True
                    return False
                kind, arr, srcA, ti, j0, n = gchunks[gi]
                ncols = n * 128 if kind == "gu" else n * 256
                slot = gi % 2
                base = GSLOT[slot]
                idxs = AR_IDX[:, 1024 * slot:1024 * slot + ncols]
                idxb = bfXNT[:, base:base + ncols]
                acc32 = f32XNT[:, (base + 1024) // 2:(base + 1024) // 2 + ncols]
                mask = bfXNT[:, base + 3072:base + 3072 + ncols]
                if self.k == -1:
                    while self.loaded < min(gi + 2, len(gchunks)):
                        self._load(self.loaded)

                    def cast(e, idxs=idxs, idxb=idxb, acc32=acc32):
                        e.tensor_copy(idxb, idxs)
                        return e.memset(acc32, 0.0)
                    op("vector", "dv", cast,
                       waits=[("qg", chunk_ldq[gi]), ("dv", dv_lut)])
                    self.k = 0
                    return True
                if self.k < 16:
                    k_ = self.k

                    def trip(e, idxb=idxb, acc32=acc32, mask=mask, k_=k_):
                        e.tensor_scalar(out=mask, in0=idxb, scalar1=float(k_),
                                        scalar2=None, op0=ALU.is_equal)
                        return e.scalar_tensor_tensor(
                            out=acc32, in0=mask, scalar=lutb[:, k_:k_ + 1],
                            in1=acc32, op0=ALU.mult, op1=ALU.add)
                    dvv = op("vector", "dv", trip)
                    self.k += 1
                    if self.k == 16:
                        self._acc_dv = dvv
                    return True
                ahs = op("scalar", "ac", lambda e, acc32=acc32, mask=mask:
                         e.activation(out=mask, in_=acc32, func=AF.Copy),
                         waits=[("dv", self._acc_dv)])
                dlo = op("vector", "dv", lambda e, acc32=acc32, mask=mask, idxb=idxb:
                         e.scalar_tensor_tensor(out=idxb, in0=acc32, scalar=1.0,
                                                in1=mask, op0=ALU.mult,
                                                op1=ALU.subtract),
                         waits=[("ac", ahs)])
                chunk_dv[gi] = dlo
                if kind == "gu":
                    dsth = stg[arr, ti * IT + j0:ti * IT + j0 + n, 0].rearrange(
                        "j p q -> p j q")
                    dstl = stg[arr, ti * IT + j0:ti * IT + j0 + n, 1].rearrange(
                        "j p q -> p j q")
                    srch = mask.rearrange("p (j q) -> p j q", j=n)
                    srcl = idxb.rearrange("p (j q) -> p j q", j=n)
                else:
                    dsth = stg[arr, 2 * j0:2 * (j0 + n), 0].rearrange("i p q -> p i q")
                    dstl = stg[arr, 2 * j0:2 * (j0 + n), 1].rearrange("i p q -> p i q")
                    srch = mask.rearrange("p (i q) -> p i q", i=2 * n)
                    srcl = idxb.rearrange("p (i q) -> p i q", i=2 * n)
                op("gpsimd", "qg", lambda e, d=dsth, s=srch:
                   e.dma_start(out=d, in_=s), k=16, waits=[("dv", dlo)])
                stq = op("gpsimd", "qg", lambda e, d=dstl, s=srcl:
                         e.dma_start(out=d, in_=s), k=16)
                self.gi += 1
                self.k = -1
                if gi + 1 == len(gchunks) or gchunks[gi + 1][1] != arr:
                    cc_arr[arr] = op("gpsimd", "cc", lambda e, arr=arr:
                                     e.collective_compute(
                                         "AllGather", ALU.bypass,
                                         replica_groups=[[0, 1, 2, 3, 4, 5, 6, 7]],
                                         ins=[stg[arr]], outs=[agg[arr]]),
                                     waits=[("qg", stq)])
                return True

        chunk_dv = {}
        chunk_ldq = {}
        gat = Gather()

        def drip_gather(nmax):
            while nmax > 0 and gat.step():
                nmax -= 1

        xn2hi = bfXN2[:, 0:16384]
        xn2lo = bfXN2[:, 16384:32768]
        tmpA = AR_HALL[:, 0:2048]
        tmpB = AR_TMP[:, 4096:6144]
        SUMS = stats[:, 0:16]
        SQS = stats[:, 16:32]
        MEAN = stats[:, 32:48]
        MSQ = stats[:, 48:64]
        VAR = stats[:, 64:80]
        VPE = stats[:, 80:96]
        RINV = stats[:, 96:112]
        EX2 = stats[:, 112:128]
        RSTD = stats[:, 128:144]
        M2 = stats[:, 144:160]
        M2E = stats[:, 160:176]
        R2INV = stats[:, 176:192]
        RST2 = stats[:, 192:208]
        EPS16 = smalls[:, SM_EPS:SM_EPS + 16]
        INVD16 = smalls[:, SM_INVD:SM_INVD + 16]

        last_sq_ac = 0
        d_red = 0
        for tt in range(8):
            xs = AR_TMP[:, 2048 * (tt % 2):2048 * (tt % 2) + 2048]
            w = [("dv", d_red)] if tt >= 2 else []
            xq = op("sync", "q1", lambda e, xs=xs, tt=tt:
                    e.dma_start(out=xs, in_=x_in[tt]), k=16, waits=w)
            wA = [("q1", xq)]
            if tt >= 1:
                wA.append(("dv", d_red))
            last_sq_ac = op("scalar", "ac", lambda e, xs=xs:
                            e.activation(out=tmpB, in_=xs, func=AF.Square), waits=wA)

            def reds(e, xs=xs, tt=tt):
                e.tensor_reduce(SUMS[:, tt:tt + 1], xs, AXX, ALU.add)
                return e.tensor_reduce(SQS[:, tt:tt + 1], tmpB, AXX, ALU.add)
            d_red = op("vector", "dv", reds,
                       waits=[("q1", xq), ("ac", last_sq_ac)])
            drip_gather(12)

        def widestats1(e):
            e.tensor_tensor(MEAN, SUMS, INVD16, ALU.mult)
            e.memset(warm[:], 0.0)
            e.tensor_tensor(EX2, SQS, INVD16, ALU.mult)
            e.memset(warm[:], 0.0)
            e.tensor_tensor(MSQ, MEAN, MEAN, ALU.mult)
            e.memset(warm[:], 0.0)
            e.tensor_tensor(VAR, EX2, MSQ, ALU.subtract)
            e.memset(warm[:], 0.0)
            e.tensor_tensor(VPE, VAR, EPS16, ALU.add)
            e.memset(warm[:], 0.0)
            return e.reciprocal(RINV, VPE)
        dws1 = op("vector", "dv", widestats1, waits=[("dv", d_red)])
        a_rstd = op("scalar", "ac", lambda e:
                    e.activation(out=RSTD, in_=RINV, func=AF.Sqrt),
                    waits=[("dv", dws1)])

        def widestats2(e):
            e.tensor_tensor(M2, VAR, RINV, ALU.mult)
            e.memset(warm[:], 0.0)
            e.tensor_tensor(M2E, M2, EPS16, ALU.add)
            e.memset(warm[:], 0.0)
            return e.reciprocal(R2INV, M2E)
        dws2 = op("vector", "dv", widestats2, waits=[("ac", a_rstd)])
        a_rst2 = op("scalar", "ac", lambda e:
                    e.activation(out=RST2, in_=R2INV, func=AF.Sqrt),
                    waits=[("dv", dws2)])

        pe_after_tt = {}
        spill_qg_prev = 0
        for quarter in range(4):
            for tloc in range(2):
                tt = quarter * 2 + tloc
                xs = AR_TMP[:, 2048 * (tt % 2):2048 * (tt % 2) + 2048]
                w = [("ac", a_rst2)]
                if tt >= 2:
                    w.append(("pe", pe_after_tt[tt - 2]))
                if tloc == 0 and quarter > 0:
                    w.append(("qg", spill_qg_prev))
                xq = op("sync", "q1", lambda e, xs=xs, tt=tt:
                        e.dma_start(out=xs, in_=x_in[tt]), k=16, waits=w)

                def normalize(e, xs=xs, tt=tt):
                    e.tensor_scalar(out=tmpB, in0=xs, scalar1=MEAN[:, tt:tt + 1],
                                    scalar2=None, op0=ALU.subtract)
                    return e.tensor_scalar(out=tmpA, in0=tmpB,
                                           scalar1=RSTD[:, tt:tt + 1], scalar2=None,
                                           op0=ALU.mult)
                wN = [("q1", xq)]
                if tt >= 1:
                    wN.append(("pe", pe_after_tt[tt - 1]))
                d3 = op("vector", "dv", normalize, waits=wN)
                d5 = op("vector", "dv", lambda e, tt=tt:
                        e.tensor_scalar(out=tmpB, in0=tmpA,
                                        scalar1=RST2[:, tt:tt + 1], scalar2=None,
                                        op0=ALU.mult),
                        waits=[("dv", d3)])

                for dti in range(KT):
                    bxn, wxn = bank_acquire()
                    pexn = op("tensor", "pe", lambda e, b=bxn, dti=dti:
                              e.transpose(pslot(b)[:, 0:128],
                                          tmpA[:, 128 * dti:128 * dti + 128],
                                          identsb[:]),
                              waits=wxn + [("dv", d3), ("q1", q1_init)])
                    hq = bfBF[:, dti * 256 + 128 * tloc:dti * 256 + 128 * tloc + 128]
                    lq = bfBF[:, 4096 + dti * 256 + 128 * tloc:
                               4096 + dti * 256 + 128 * tloc + 128]
                    achi = op("scalar", "ac", lambda e, b=bxn, hq=hq:
                              e.activation(out=hq, in_=pslot(b)[:, 0:128],
                                           func=AF.Copy), waits=[("pe", pexn)])
                    dvv = op("vector", "dv", lambda e, b=bxn, hq=hq, lq=lq:
                             e.scalar_tensor_tensor(out=lq, in0=pslot(b)[:, 0:128],
                                                    scalar=1.0, in1=hq, op0=ALU.mult,
                                                    op1=ALU.subtract),
                             waits=[("pe", pexn), ("ac", achi)])
                    bank_release(bxn, "ac", achi)
                    bank_release(bxn, "dv", dvv)

                    bba, wba = bank_acquire()
                    peba = op("tensor", "pe", lambda e, b=bba, dti=dti:
                              e.transpose(pslot(b)[:, 0:128],
                                          tmpB[:, 128 * dti:128 * dti + 128],
                                          identsb[:]),
                              waits=wba + [("dv", d5)])
                    h2 = bfXN2[:, dti * 1024 + 128 * tt:dti * 1024 + 128 * tt + 128]
                    l2 = bfXN2[:, 16384 + dti * 1024 + 128 * tt:
                               16384 + dti * 1024 + 128 * tt + 128]
                    wap = smalls[:, SM_RMS + dti:SM_RMS + dti + 1]
                    aca = op("scalar", "ac", lambda e, b=bba, h2=h2, wap=wap:
                             e.activation(out=h2, in_=pslot(b)[:, 0:128], func=AF.Copy,
                                          scale=wap), waits=[("pe", peba)])

                    def evb2(e, b=bba, h2=h2, l2=l2, wap=wap):
                        return e.scalar_tensor_tensor(out=l2, in0=pslot(b)[:, 0:128],
                                                      scalar=wap, in1=h2,
                                                      op0=ALU.mult, op1=ALU.subtract)
                    dvb = op("vector", "dv", evb2, waits=[("pe", peba), ("ac", aca)])
                    bank_release(bba, "ac", aca)
                    bank_release(bba, "dv", dvb)
                pe_after_tt[tt] = cnt["pe"]
            dvq = cnt["dv"]
            for hl in range(2):
                srcs = bfBF[:, 4096 * hl:4096 * hl + 4096].rearrange(
                    "p (d t) -> p d t", d=KT)
                dsts = xnsp[hl, :, :, 256 * quarter:256 * quarter + 256].rearrange(
                    "d p t -> p d t")
                spill_qg_prev = op("gpsimd", "qg", lambda e, srcs=srcs, dsts=dsts:
                                   e.dma_start(out=dsts, in_=srcs), k=16,
                                   waits=[("dv", dvq)] if hl == 0 else [])

        phaseA_dv = cnt["dv"]
        phaseA_ac = cnt["ac"]
        xnsp_done_qg = cnt["qg"]

        FZ_F = [0, 1024]
        FZ_Z = [2048, 3072]
        bfTMP = AR_TMP[:].bitcast(BF16)
        WSLOT_AP = [bfBF[:, 4096:8192], bfBF[:, 8192:12288], bfTMP[:, 0:4096]]
        wslot_free = [[], [], []]
        w32_free = [[]]
        slot_rr = [0]

        def load_convert_w(wname, j, slot, extra_wait=()):
            w32 = AR_TMP[:, 2048:4096]
            src = wT[wname][:, :, 128 * j:128 * j + 128]
            w = list(w32_free[0]) + list(extra_wait)
            w32_free[0] = []
            lq = op("sync", "q1", lambda e, w32=w32, src=src:
                    e.dma_start(out=w32.rearrange("p (k c) -> p k c", k=KT),
                                in_=src.rearrange("k p c -> p k c")),
                    k=16, waits=w)
            sap = WSLOT_AP[slot]
            hi = sap[:, 0:2048]
            lo = sap[:, 2048:4096]
            wsl = list(wslot_free[slot])
            wslot_free[slot] = []
            ah = op("scalar", "ac", lambda e, hi=hi, w32=w32:
                    e.activation(out=hi, in_=w32, func=AF.Copy),
                    waits=[("q1", lq)] + wsl)
            dl = op("vector", "dv", lambda e, lo=lo, w32=w32, hi=hi:
                    e.tensor_tensor(lo, w32, hi, ALU.subtract),
                    waits=[("q1", lq), ("ac", ah)])
            w32_free[0] = [("ac", ah), ("dv", dl)]
            return hi, lo, ah, dl

        def mm_chain3(bank, hi, lo, xhi, xlo, th, waits):
            def chain(e):
                r = None
                for k in range(KT):
                    hk = hi[:, 128 * k:128 * k + 128]
                    lk = lo[:, 128 * k:128 * k + 128]
                    xh = xhi[:, 1024 * k + 512 * th:1024 * k + 512 * th + 512]
                    xl = xlo[:, 1024 * k + 512 * th:1024 * k + 512 * th + 512]
                    e.matmul(pslot(bank), hk, xh, start=(k == 0), stop=False)
                    e.matmul(pslot(bank), hk, xl, start=False, stop=False)
                    r = e.matmul(pslot(bank), lk, xh, start=False, stop=(k == KT - 1))
                return r
            return op("tensor", "pe", chain, waits=waits)

        phase1_ready = [("dv", phaseA_dv), ("ac", phaseA_ac), ("qg", xnsp_done_qg)]
        scan_done_dv = 0
        zgs_guard = [0, 0]

        def p1_load(j, extra=()):
            trip = {}
            for wn in ["wzg", "wz", "wf"]:
                slot = slot_rr[0] % 3
                slot_rr[0] += 1
                trip[wn] = (load_convert_w(wn, j, slot, extra_wait=extra), slot)
            return trip
        pending1 = p1_load(0, extra=phase1_ready)
        for j in range(KT):
            fb = bfBF[:, FZ_F[j % 2]:FZ_F[j % 2] + 1024]
            zb = bfBF[:, FZ_Z[j % 2]:FZ_Z[j % 2] + 1024]
            fz_guard = [("dv", scan_done_dv)] if j >= 2 else []
            per_w = {}
            for wn in ["wzg", "wz", "wf"]:
                (hi, lo, ah, dl), slot = pending1[wn]
                pe_chain = []
                for th in range(2):
                    b, bw = bank_acquire()
                    pv = mm_chain3(b, hi, lo, xn2hi, xn2lo, th,
                                   waits=bw + [("ac", ah), ("dv", dl)] +
                                   (phase1_ready if (j == 0 and wn == "wzg" and th == 0)
                                    else []))
                    pe_chain.append((b, pv))
                wslot_free[slot] = [("pe", pe_chain[-1][1])]
                per_w[wn] = pe_chain
            if j + 1 < KT:
                pending1 = p1_load(j + 1)
            for th in range(2):
                bzg, pzg = per_w["wzg"][th]
                bz, pz = per_w["wz"][th]
                bf_, pf = per_w["wf"][th]
                zgs = AR_TMP[:, 4096 + 512 * th:4096 + 512 * th + 512]
                a_zg = op("scalar", "ac", lambda e, b=bzg, zgs=zgs:
                          e.activation(out=zgs, in_=pslot(b), func=AF.Sigmoid),
                          waits=[("pe", pzg), ("dv", zgs_guard[th])])
                bank_release(bzg, "ac", a_zg)
                d_z = op("vector", "dv", lambda e, b=bz, zgs=zgs, zb=zb, th=th:
                         e.tensor_tensor(zb[:, 512 * th:512 * th + 512], pslot(b), zgs,
                                         ALU.mult),
                         waits=[("pe", pz), ("ac", a_zg)])
                bank_release(bz, "dv", d_z)
                zgs_guard[th] = d_z
                drip_gather(4)
                fbias = smalls[:, SM_WFB + j:SM_WFB + j + 1]
                a_f = op("scalar", "ac", lambda e, b=bf_, fb=fb, th=th, fbias=fbias:
                         e.activation(out=fb[:, 512 * th:512 * th + 512], in_=pslot(b),
                                      func=AF.Sigmoid, bias=fbias),
                         waits=[("pe", pf)] + (fz_guard if th == 0 else []))
                bank_release(bf_, "ac", a_f)
            hj = bfHALL[:, 1024 * j:1024 * j + 1024]
            pj = bfHALL[:, 16384 + 1024 * j:16384 + 1024 * j + 1024]
            def scans(e, fb=fb, zb=zb, hj=hj, pj=pj, j=j):
                e.tensor_tensor_scan(hj, fb, zb, 0.0, ALU.mult, ALU.add)
                return e.tensor_tensor_scan(pj, fb, zerob[:], 1.0, ALU.mult, ALU.add)
            scan_done_dv = op("vector", "dv", scans, waits=[("ac", cnt["ac"])])
            drip_gather(10)

        all_scans_dv = scan_done_dv

        def p2_load(j):
            trip = {}
            for wn in ["wga", "wgb"]:
                slot = slot_rr[0] % 3
                slot_rr[0] += 1
                trip[wn] = (load_convert_w(wn, j, slot), slot)
            return trip
        pending2 = p2_load(0)

        bsrc = bfHALL[:, 0:16384].rearrange(
            "p (j t) -> p j t", j=KT)[:, :, 1023:1024].squeeze(2)
        all_scans_dv = op("vector", "dv", lambda e, bsrc=bsrc:
                          e.tensor_copy(bndsb[:, 0:16], bsrc))
        bq = op("gpsimd", "qg", lambda e:
                e.dma_start(out=bnd_in, in_=bndsb[:, 0:16]),
                k=16, waits=[("dv", all_scans_dv)])
        cc1 = op("gpsimd", "cc", lambda e: e.collective_compute(
            "AllGather", ALU.bypass,
            replica_groups=[[0, 1], [2, 3], [4, 5], [6, 7]],
            ins=[bnd_in], outs=[bnd_out]), waits=[("qg", bq)])
        hbq = op("gpsimd", "qg", lambda e: e.dma_start(out=bndsb[:, 64:80],
                                                       in_=bnd_out[0]),
                 k=16, waits=[("cc", cc1)])

        corr_holder = [0]

        def emit_boundary_tail():
            hbm_ac = op("scalar", "ac", lambda e:
                        e.activation(out=bndsb[:, 128:144], in_=bndsb[:, 64:80],
                                     func=AF.Copy,
                                     scale=smalls[:, SM_MSK:SM_MSK + 1]),
                        waits=[("qg", hbq)])
            hbm_dv = op("vector", "dv", lambda e: e.memset(warm[:], 0.0),
                        waits=[("ac", hbm_ac)])

            def corrections(e):
                r = None
                for j in range(KT):
                    hj = bfHALL[:, 1024 * j:1024 * j + 1024]
                    pj = bfHALL[:, 16384 + 1024 * j:16384 + 1024 * j + 1024]
                    r = e.scalar_tensor_tensor(out=hj, in0=pj,
                                               scalar=bndsb[:, 128 + j:129 + j],
                                               in1=hj, op0=ALU.mult, op1=ALU.add)
                return r
            corr_holder[0] = op("vector", "dv", corrections, waits=[("dv", hbm_dv)])

        gas_guard = list(zgs_guard)
        for j in range(KT):
            per_w = {}
            for wn in ["wga", "wgb"]:
                (hi, lo, ah, dl), slot = pending2[wn]
                pe_chain = []
                for th in range(2):
                    b, bw = bank_acquire()
                    pv = mm_chain3(b, hi, lo, xn2hi, xn2lo, th,
                                   waits=bw + [("ac", ah), ("dv", dl)])
                    pe_chain.append((b, pv))
                wslot_free[slot] = [("pe", pe_chain[-1][1])]
                per_w[wn] = pe_chain
            if j + 1 < KT:
                pending2 = p2_load(j + 1)
            ogb = bfBF[:, FZ_F[j % 2]:FZ_F[j % 2] + 1024]
            for th in range(2):
                bga, pga = per_w["wga"][th]
                bgb, pgb = per_w["wgb"][th]
                gas = AR_TMP[:, 4096 + 512 * th:4096 + 512 * th + 512]
                a_ga = op("scalar", "ac", lambda e, b=bga, gas=gas:
                          e.activation(out=gas, in_=pslot(b), func=AF.Silu),
                          waits=[("pe", pga), ("dv", gas_guard[th])])
                bank_release(bga, "ac", a_ga)
                d_og = op("vector", "dv", lambda e, b=bgb, gas=gas, ogb=ogb, th=th:
                          e.tensor_tensor(ogb[:, 512 * th:512 * th + 512], pslot(b),
                                          gas, ALU.mult),
                          waits=[("pe", pgb), ("ac", a_ga)])
                bank_release(bgb, "dv", d_og)
                gas_guard[th] = d_og
                drip_gather(12)
            if j == 0:
                emit_boundary_tail()
            if j >= 1:
                jp = j - 1
                gp = bfHALL[:, 16384 + 1024 * jp:16384 + 1024 * jp + 1024]
                hp = bfHALL[:, 1024 * jp:1024 * jp + 1024]
                ogp = bfBF[:, FZ_F[jp % 2]:FZ_F[jp % 2] + 1024]
                op("vector", "dv", lambda e, gp=gp, hp=hp, ogp=ogp:
                   e.tensor_tensor(gp, ogp, hp, ALU.mult),
                   waits=[("dv", corr_holder[0])])
        op("vector", "dv", lambda e: e.memset(warm[:], 0.0))
        gp = bfHALL[:, 16384 + 1024 * 15:16384 + 1024 * 15 + 1024]
        hp = bfHALL[:, 1024 * 15:1024 * 15 + 1024]
        ogp = bfBF[:, FZ_F[15 % 2]:FZ_F[15 % 2] + 1024]
        op("vector", "dv", lambda e, gp=gp, hp=hp, ogp=ogp:
           e.tensor_tensor(gp, ogp, hp, ALU.mult))
        all_gated_dv = cnt["dv"]
        drip_gather(len(gchunks) * 20)

        pe_before_wout = cnt["pe"]
        guslot_free = [[], []]
        gu_loads = {}

        def emit_gu_load(i, arr, extra=()):
            slot = (2 * i + arr) % 2
            lw = list(guslot_free[slot]) + list(extra)
            guslot_free[slot] = []
            dstW = bfXNT[:, 4096 * slot:4096 * slot + 4096]
            dW5 = dstW.rearrange("p (r ti hl c) -> p r ti hl c", r=8, ti=2, hl=2)
            lq = None
            for ti in range(2):
                for hl in range(2):
                    srcW = agg[arr, :, i + IT * ti, hl].rearrange("r p c -> p r c")
                    dstWt = dW5[:, :, ti, hl]
                    lq = op("gpsimd", "qg",
                            lambda e, dstWt=dstWt, srcW=srcW:
                            e.dma_start(out=dstWt, in_=srcW),
                            k=16,
                            waits=(lw + [("cc", cc_arr[arr]),
                                         ("cc", cc_arr[2])])
                            if (ti == 0 and hl == 0) else [])
            gu_loads[(i, arr)] = (slot, lq)

        gated = bfHALL[:, 16384:32768]
        emit_gu_load(0, 0)
        emit_gu_load(0, 1)
        ysp_guard = [0, 0]

        def pw_load(j):
            slot = slot_rr[0] % 3
            slot_rr[0] += 1
            return load_convert_w("wo", j, slot), slot
        pendw = pw_load(0)
        for j in range(KT):
            (hi, lo, ah, dl), slot = pendw
            for th in range(2):
                b, bw = bank_acquire()

                def chain2(e, b=b, hi=hi, lo=lo, th=th):
                    r = None
                    for k in range(KT):
                        hk = hi[:, 128 * k:128 * k + 128]
                        lk = lo[:, 128 * k:128 * k + 128]
                        g = gated[:, 1024 * k + 512 * th:1024 * k + 512 * th + 512]
                        e.matmul(pslot(b), hk, g, start=(k == 0), stop=False)
                        r = e.matmul(pslot(b), lk, g, start=False, stop=(k == KT - 1))
                    return r
                pv = op("tensor", "pe", chain2,
                        waits=bw + [("ac", ah), ("dv", dl)] +
                        ([("dv", all_gated_dv)] if (j == 0 and th == 0) else []))
                if th == 1:
                    wslot_free[slot] = [("pe", pv)]
                    if j + 1 < KT:
                        pendw = pw_load(j + 1)
                ybuf = AR_TMP[:, 5120 + 512 * th:5120 + 512 * th + 512]
                bias = smalls[:, SM_WOB + j:SM_WOB + j + 1]
                av = op("vector", "dv", lambda e, b=b, ybuf=ybuf, bias=bias:
                        e.tensor_scalar(out=ybuf, in0=pslot(b), scalar1=bias,
                                        scalar2=None, op0=ALU.add),
                        waits=[("pe", pv), ("qg", ysp_guard[th])])
                bank_release(b, "dv", av)
                ysp_guard[th] = op("gpsimd", "qg", lambda e, ybuf=ybuf, j=j, th=th:
                                   e.dma_start(out=ysp[j, :, 512 * th:512 * th + 512],
                                               in_=ybuf),
                                   k=16, waits=[("dv", av)])
        ysp_qg = cnt["qg"]
        wout_pe = cnt["pe"]

        xn_rel = 0
        for hl in range(2):
            for dti in range(KT):
                if hl == 0:
                    dst = bfHALL[:, 1024 * dti:1024 * dti + 1024]
                    wlist = [("dv", all_gated_dv)] if dti == 0 else []
                else:
                    dst = bfXN2[:, 1024 * dti:1024 * dti + 1024]
                    wlist = [("pe", pe_before_wout)] if dti == 0 else []
                xn_rel = op("sync", "q1", lambda e, dst=dst, hl=hl, dti=dti:
                            e.dma_start(out=dst, in_=xnsp[hl, dti]), k=16, waits=wlist)
        xnhi_sb = bfHALL[:, 0:16384]
        xnlo_sb = bfXN2[:, 0:16384]

        def ycol(j, th):
            if j < 8:
                return AR_XN2[:, 8192 + 1024 * j + 512 * th:
                              8192 + 1024 * j + 512 * th + 512]
            return AR_HALL[:, 8192 + 1024 * (j - 8) + 512 * th:
                           8192 + 1024 * (j - 8) + 512 * th + 512]

        def yfull(j):
            if j < 8:
                return AR_XN2[:, 8192 + 1024 * j:8192 + 1024 * j + 1024]
            return AR_HALL[:, 8192 + 1024 * (j - 8):8192 + 1024 * (j - 8) + 1024]

        dn_free = [[], [], []]
        chunks = []
        i0 = 0
        while i0 < IT:
            chunks.append((i0, min(DCHUNK, IT - i0)))
            i0 += DCHUNK

        first_gu = [True]
        h_guard = {}
        gas_guard2 = list(gas_guard)
        htmp_guard = [dict(th=0), None]
        htmp_first = [True, True]
        dve_after_chunk = 0
        for ci, (c0, cn) in enumerate(chunks):
            gate_pair = up_pair = None
            h_ready = []
            for ii in range(cn):
                i = c0 + ii
                pair = {}
                for arr in (0, 1):
                    if (i, arr) not in gu_loads:
                        emit_gu_load(i, arr)
                    slot, lq = gu_loads.pop((i, arr))
                    dstW = bfXNT[:, 4096 * slot:4096 * slot + 4096]
                    wsl = dstW
                    pe_pair = []
                    for th in range(2):
                        b, bw = bank_acquire()

                        def chaing(e, b=b, wsl=wsl, th=th):
                            r = None
                            for k in range(KT):
                                hk = wsl[:, 256 * k:256 * k + 128]
                                lk = wsl[:, 256 * k + 128:256 * k + 256]
                                xh = xnhi_sb[:, 1024 * k + 512 * th:
                                             1024 * k + 512 * th + 512]
                                xl = xnlo_sb[:, 1024 * k + 512 * th:
                                             1024 * k + 512 * th + 512]
                                e.matmul(pslot(b), hk, xh, start=(k == 0), stop=False)
                                e.matmul(pslot(b), hk, xl, start=False, stop=False)
                                r = e.matmul(pslot(b), lk, xh, start=False,
                                             stop=(k == KT - 1))
                            return r
                        pv = op("tensor", "pe", chaing,
                                waits=bw + [("qg", lq)] +
                                ([("q1", xn_rel)] if (ci == 0 and ii == 0 and th == 0
                                                      and arr == 0) else []))
                        pe_pair.append((b, pv))
                    guslot_free[slot] = [("pe", pe_pair[1][1])]
                    pair[arr] = pe_pair
                gate_pair, up_pair = pair[0], pair[1]
                for th in range(2):
                    bg, pg = gate_pair[th]
                    bu, pu = up_pair[th]
                    gas = AR_TMP[:, 4096 + 512 * th:4096 + 512 * th + 512]
                    ag = op("scalar", "ac", lambda e, b=bg, gas=gas:
                            e.activation(out=gas, in_=pslot(b), func=AF.Silu),
                            waits=[("pe", pg), ("dv", gas_guard2[th])])
                    bank_release(bg, "ac", ag)
                    htmp = AR_TMP[:, 5120 + 512 * th:5120 + 512 * th + 512]
                    hhi = bfBF[:, 4096 + 2048 * ii + 512 * th:
                               4096 + 2048 * ii + 512 * th + 512]
                    hlo = bfBF[:, 4096 + 2048 * ii + 1024 + 512 * th:
                               4096 + 2048 * ii + 1024 + 512 * th + 512]
                    hw = [("pe", pu), ("ac", ag)]
                    if (ii, th) in h_guard:
                        hw.append(("pe", h_guard[(ii, th)]))
                    if htmp_first[th]:
                        hw.append(("qg", ysp_guard[th]))
                        htmp_first[th] = False

                    dm = op("vector", "dv", lambda e, b=bu, gas=gas, htmp=htmp:
                            e.tensor_tensor(htmp, pslot(b), gas, ALU.mult), waits=hw)
                    ah2 = op("scalar", "ac", lambda e, htmp=htmp, hhi=hhi:
                             e.activation(out=hhi, in_=htmp, func=AF.Copy),
                             waits=[("dv", dm)])
                    dh = op("vector", "dv", lambda e, htmp=htmp, hhi=hhi, hlo=hlo:
                            e.scalar_tensor_tensor(out=hlo, in0=htmp, scalar=1.0,
                                                   in1=hhi, op0=ALU.mult,
                                                   op1=ALU.subtract),
                            waits=[("ac", ah2)])
                    bank_release(bu, "dv", dm)
                    gas_guard2[th] = dh
                    h_ready.append(dh)
            h_all_ready = max(h_ready)
            for j in range(KT):
                dslot = j % 3
                dw = list(dn_free[dslot])
                dn_free[dslot] = []
                dsb = bfBF[:, 1024 * dslot:1024 * dslot + 256 * cn]
                dsb4 = dsb.rearrange("p (i hl c) -> p i hl c", i=cn, hl=2)
                for hl in range(2):
                    srcD = agg[2, j // 2, 2 * c0 + (j % 2):2 * (c0 + cn):2, hl]
                    dwx = (dw + ([("cc", cc_arr[2]), ("dv", all_gated_dv)]
                                 if ci == 0 else [])) if hl == 0 else []
                    lq = op("gpsimd", "qg",
                            lambda e, dsb4=dsb4, srcD=srcD, hl=hl:
                            e.dma_start(out=dsb4[:, :, hl],
                                        in_=srcD.rearrange("i p c -> p i c")),
                            k=16, waits=dwx)
                last_pv = 0
                for th in range(2):
                    b, bw = bank_acquire()

                    def chaind(e, b=b, dsb=dsb, th=th, cn=cn):
                        r = None
                        for ii in range(cn):
                            hk = dsb[:, 256 * ii:256 * ii + 128]
                            lk = dsb[:, 256 * ii + 128:256 * ii + 256]
                            xh = bfBF[:, 4096 + 2048 * ii + 512 * th:
                                      4096 + 2048 * ii + 512 * th + 512]
                            xl = bfBF[:, 4096 + 2048 * ii + 1024 + 512 * th:
                                      4096 + 2048 * ii + 1024 + 512 * th + 512]
                            e.matmul(pslot(b), hk, xh, start=(ii == 0), stop=False)
                            e.matmul(pslot(b), hk, xl, start=False, stop=False)
                            r = e.matmul(pslot(b), lk, xh, start=False,
                                         stop=(ii == cn - 1))
                        return r
                    pv = op("tensor", "pe", chaind,
                            waits=bw + [("qg", lq), ("dv", h_all_ready)])
                    last_pv = pv
                    yc = ycol(j, th)
                    if ci == 0:
                        dy = op("vector", "dv", lambda e, b=b, yc=yc:
                                e.tensor_copy(yc, pslot(b)), waits=[("pe", pv)])
                    else:
                        dy = op("vector", "dv", lambda e, b=b, yc=yc:
                                e.scalar_tensor_tensor(out=yc, in0=pslot(b),
                                                       scalar=1.0, in1=yc,
                                                       op0=ALU.mult, op1=ALU.add),
                                waits=[("pe", pv)])
                    bank_release(b, "dv", dy)
                dn_free[dslot] = [("pe", last_pv)]
            for ii in range(DCHUNK):
                for th in range(2):
                    h_guard[(ii, th)] = cnt["pe"]
            dve_after_chunk = cnt["dv"]
        ffn_dv = cnt["dv"]

        g_buf = AR_TMP[:, 0:1024]
        r_done = {}
        prev_r = ffn_dv
        for j in range(KT):
            ya = AR_TMP[:, 1024 + 1024 * (j % 2):1024 + 1024 * (j % 2) + 1024]
            w = [("dv", r_done[j - 2])] if j >= 2 else []
            yq = op("sync", "q1", lambda e, ya=ya, j=j:
                    e.dma_start(out=ya, in_=ysp[j]), k=16,
                    waits=w + ([("qg", ysp_qg)] if j == 0 else []))
            aj = op("scalar", "ac", lambda e, j=j:
                    e.activation(out=g_buf, in_=yfull(j), func=AF.Silu),
                    waits=[("dv", ffn_dv if j == 0 else prev_r)])

            def rops(e, ya=ya, j=j):
                yj = yfull(j)
                e.scalar_tensor_tensor(out=ya, in0=xnhi_sb[:, 1024 * j:1024 * j + 1024],
                                       scalar=1.0, in1=ya, op0=ALU.mult, op1=ALU.add)
                e.scalar_tensor_tensor(out=ya, in0=xnlo_sb[:, 1024 * j:1024 * j + 1024],
                                       scalar=1.0, in1=ya, op0=ALU.mult, op1=ALU.add)
                return e.scalar_tensor_tensor(out=yj, in0=g_buf, scalar=1.0, in1=ya,
                                              op0=ALU.mult, op1=ALU.mult)
            prev_r = op("vector", "dv", rops,
                        waits=[("q1", yq), ("ac", aj), ("dv", ffn_dv)])
            r_done[j] = prev_r
        all_r_dv = prev_r

        out_q_map = {}
        prev_out_dv = all_r_dv
        for tt in range(8):
            xb = AR_TMP[:, 3072:5120]
            w = [("dv", prev_out_dv)] if tt >= 1 else [("dv", all_r_dv)]
            xq = op("sync", "q1", lambda e, xb=xb, tt=tt:
                    e.dma_start(out=xb, in_=x_in[tt]), k=16, waits=w)
            outsb = AR_XN2[:, 2048 * (tt % 2):2048 * (tt % 2) + 2048]
            grp_dv = 0
            for g4 in range(4):
                b, bw = bank_acquire()

                def tchain(e, b=b, tt=tt, g4=g4):
                    r = None
                    for jj in range(4):
                        j = 4 * g4 + jj
                        r = e.transpose(
                            pslot(b)[:, 128 * jj:128 * jj + 128],
                            yfull(j)[:, 128 * tt:128 * tt + 128],
                            identsb[:])
                    return r
                pv = op("tensor", "pe", tchain, waits=bw + [("dv", all_r_dv)])
                wv = [("pe", pv), ("q1", xq)]
                if tt >= 2 and g4 == 0:
                    wv.append(("qg", out_q_map[tt - 2]))
                dv = op("vector", "dv", lambda e, b=b, outsb=outsb, xb=xb, g4=g4:
                        e.scalar_tensor_tensor(out=outsb[:, 512 * g4:512 * g4 + 512],
                                               in0=pslot(b), scalar=1.0,
                                               in1=xb[:, 512 * g4:512 * g4 + 512],
                                               op0=ALU.mult, op1=ALU.add),
                        waits=wv)
                bank_release(b, "dv", dv)
                grp_dv = dv
            prev_out_dv = grp_dv
            out_q_map[tt] = op("gpsimd", "qg", lambda e, outsb=outsb, tt=tt:
                               e.dma_start(out=out_dram[tt], in_=outsb),
                               k=16, waits=[("dv", grp_dv)])
        wait("gpsimd", "qg", cnt["qg"])
        wait("sync", "q1", cnt["q1"])

        with nc.Block() as block:
            @block.sync
            def _(e):
                for f in prog["sync"]:
                    f(e)

            @block.gpsimd
            def _(e):
                for f in prog["gpsimd"]:
                    f(e)

            @block.vector
            def _(e):
                for f in prog["vector"]:
                    f(e)

            @block.scalar
            def _(e):
                for f in prog["scalar"]:
                    f(e)

            @block.tensor
            def _(e):
                for f in prog["tensor"]:
                    f(e)
    return nc


def kernel(L0_attn_x, L0_attn_rms_w, L0_attn_Wf_w, L0_attn_Wf_b, L0_attn_Wzg_w,
           L0_attn_Wz_w, L0_attn_Wga_w, L0_attn_Wgb_w, L0_attn_Wout_w, L0_attn_Wout_b,
           L0_ffn_lut, L0_ffn_gate_idx, L0_ffn_up_idx, L0_ffn_down_idx):
    x = np.ascontiguousarray(np.asarray(L0_attn_x, dtype=np.float32))
    B, Tfull, Dm = x.shape
    xs = x.reshape(B, 2, T, Dm)

    def prepT(w):
        return np.ascontiguousarray(np.asarray(w, np.float32).T).reshape(KT, 128, D)

    wfT = prepT(L0_attn_Wf_w)
    wzgT = prepT(L0_attn_Wzg_w)
    wzT = prepT(L0_attn_Wz_w)
    wgaT = prepT(L0_attn_Wga_w)
    wgbT = prepT(L0_attn_Wgb_w)
    woT = prepT(L0_attn_Wout_w)

    rms_w = np.asarray(L0_attn_rms_w, np.float32)
    wf_b = np.asarray(L0_attn_Wf_b, np.float32)
    wout_b = np.asarray(L0_attn_Wout_b, np.float32)
    lut = np.asarray(L0_ffn_lut, np.float32)
    gidxT = np.ascontiguousarray(np.asarray(L0_ffn_gate_idx, np.int32).T)
    uidxT = np.ascontiguousarray(np.asarray(L0_ffn_up_idx, np.int32).T)
    didxT = np.ascontiguousarray(np.asarray(L0_ffn_down_idx, np.int32).T)

    ident = np.eye(128, dtype=np.float32)
    lut128 = np.tile(lut.reshape(1, 16), (128, 1)).astype(np.float32)

    nc = build_nc()
    in_maps = []
    for c in range(8):
        b, half = c // 2, c % 2
        smalls = np.zeros((128, 128), np.float32)
        smalls[:, 0:16] = rms_w.reshape(16, 128).T
        smalls[:, 16:32] = wf_b.reshape(16, 128).T
        smalls[:, 32:48] = wout_b.reshape(16, 128).T
        smalls[:, 48] = 1.0 if half == 1 else 0.0
        smalls[:, 64:80] = EPS
        smalls[:, 80:96] = 1.0 / D
        gs = np.ascontiguousarray(gidxT[256 * c:256 * c + 256]).reshape(2, 128, INTER)
        us = np.ascontiguousarray(uidxT[256 * c:256 * c + 256]).reshape(2, 128, INTER)
        ds = np.ascontiguousarray(didxT[:, 256 * c:256 * c + 256]).reshape(IT, 128, 256)
        in_maps.append({
            "x_in": np.ascontiguousarray(xs[b, half]).reshape(8, 128, D),
            "wfT": wfT, "wzgT": wzgT, "wzT": wzT, "wgaT": wgaT, "wgbT": wgbT,
            "woT": woT,
            "smalls": smalls, "lut128": lut128, "ident": ident,
            "gidx": gs, "uidx": us, "didx": ds,
        })
    res = run_bass_kernel_spmd(nc, in_maps, core_ids=list(range(8)))
    out = np.empty((B, Tfull, Dm), np.float32)
    for c in range(8):
        b, half = c // 2, c % 2
        out[b, T * half:T * half + T] = res.results[c]["out"].reshape(T, Dm)
    return out


# version 14 (final)
# speedup vs baseline: 1.1370x; 1.0039x over previous
import numpy as np
from contextlib import ExitStack

import concourse.bass as bass
import concourse.mybir as mybir
from concourse.bass_utils import run_bass_kernel_spmd

F32 = mybir.dt.float32
BF16 = mybir.dt.bfloat16
I32 = mybir.dt.int32
AF = mybir.ActivationFunctionType
ALU = mybir.AluOpType
AXX = mybir.AxisListType.X

D = 2048
S = 2048
INTER = 5504
T = 1024
KT = 16
IT = 43
EPS = 1e-5
DCHUNK = 4

XN2_N = 16384
HALL_N = 16384
BF_N = 6144
XNT_N = 4096
TMP_N = 6144
IDXN = 2048


def build_nc():
    es = ExitStack()
    nc = bass.Bass("TRN2", target_bir_lowering=False)

    x_in = nc.dram_tensor("x_in", [8, 128, D], F32, kind="ExternalInput").ap()
    wT = {}
    for nm in ["wf", "wzg", "wz", "wga", "wgb", "wo"]:
        wT[nm] = nc.dram_tensor(nm + "T", [KT, 128, D], F32, kind="ExternalInput").ap()
    smalls_in = nc.dram_tensor("smalls", [128, 128], F32, kind="ExternalInput").ap()
    lut_in = nc.dram_tensor("lut128", [128, 16], F32, kind="ExternalInput").ap()
    ident_in = nc.dram_tensor("ident", [128, 128], F32, kind="ExternalInput").ap()
    gidx_in = nc.dram_tensor("gidx", [2, 128, INTER], I32, kind="ExternalInput").ap()
    uidx_in = nc.dram_tensor("uidx", [2, 128, INTER], I32, kind="ExternalInput").ap()
    didx_in = nc.dram_tensor("didx", [IT, 128, 256], I32, kind="ExternalInput").ap()
    out_dram = nc.dram_tensor("out", [8, 128, D], F32, kind="ExternalOutput").ap()

    xnsp = nc.dram_tensor("xnsp", [2, KT, 128, T], BF16, kind="Internal").ap()
    ysp = nc.dram_tensor("ysp", [KT, 128, T], F32, kind="Internal").ap()
    stg = nc.dram_tensor("stg", [3, 86, 2, 128, 128], BF16, kind="Internal").ap()
    agg = nc.dram_tensor("agg", [3, 8, 86, 2, 128, 128], BF16, kind="Internal",
                     addr_space="Shared").ap()
    bnd_in = nc.dram_tensor("bnd_in", [128, 16], BF16, kind="Internal").ap()
    bnd_out = nc.dram_tensor("bnd_out", [2, 128, 16], BF16, kind="Internal").ap()

    with es:
        AR_XN2 = es.enter_context(nc.sbuf_tensor("AR_XN2", [128, XN2_N], F32))
        AR_HALL = es.enter_context(nc.sbuf_tensor("AR_HALL", [128, HALL_N], F32))
        AR_BF = es.enter_context(nc.sbuf_tensor("AR_BF", [128, BF_N], F32))
        AR_XNT = es.enter_context(nc.sbuf_tensor("AR_XNT", [128, XNT_N], F32))
        AR_TMP = es.enter_context(nc.sbuf_tensor("AR_TMP", [128, TMP_N], F32))
        AR_IDX = es.enter_context(nc.sbuf_tensor("AR_IDX", [128, IDXN], I32))
        smalls = es.enter_context(nc.sbuf_tensor("smalls_sb", [128, 128], F32))
        lutb = es.enter_context(nc.sbuf_tensor("lutb", [128, 16], F32))
        luthi = es.enter_context(nc.sbuf_tensor("luthi", [128, 16], F32))
        lutlo = es.enter_context(nc.sbuf_tensor("lutlo", [128, 16], F32))
        identsb = es.enter_context(nc.sbuf_tensor("identsb", [128, 128], F32))
        stats = es.enter_context(nc.sbuf_tensor("stats", [128, 256], F32))
        zerob = es.enter_context(nc.sbuf_tensor("zerob", [128, 1024], BF16))
        bndsb = es.enter_context(nc.sbuf_tensor("bndsb", [128, 256], BF16))
        warm = es.enter_context(nc.sbuf_tensor("warm", [128, 256], F32))
        psum = es.enter_context(nc.psum_tensor("psum", [128, 4096], F32))

        sem_q1 = es.enter_context(nc.semaphore("q1"))
        sem_qg = es.enter_context(nc.semaphore("qg"))
        sem_pe = es.enter_context(nc.semaphore("pe"))
        sem_dv = es.enter_context(nc.semaphore("dv"))
        sem_ac = es.enter_context(nc.semaphore("ac"))
        sem_cc = es.enter_context(nc.semaphore("cc"))

        bfBF = AR_BF[:].bitcast(BF16)
        bfXNT = AR_XNT[:].bitcast(BF16)
        bfXN2 = AR_XN2[:].bitcast(BF16)
        bfHALL = AR_HALL[:].bitcast(BF16)

        prog = {k: [] for k in ["sync", "gpsimd", "vector", "scalar", "tensor"]}
        cnt = {"q1": 0, "qg": 0, "pe": 0, "dv": 0, "ac": 0, "cc": 0}
        sems = {"q1": sem_q1, "qg": sem_qg, "pe": sem_pe, "dv": sem_dv,
                "ac": sem_ac, "cc": sem_cc}

        def emit(eng, fn):
            prog[eng].append(fn)

        def op(eng, semname, emit_fn, k=1, waits=()):
            for wsem, wval in waits:
                wh, wv = sems[wsem], int(wval)
                if wv > 0:
                    emit(eng, (lambda e, wh=wh, wv=wv: e.wait_ge(wh, wv)))
            cnt[semname] += k
            sh = sems[semname]
            emit(eng, (lambda e, fn=emit_fn, sh=sh, k=k: fn(e).then_inc(sh, k)))
            return cnt[semname]

        def wait(eng, semname, value):
            wh, wv = sems[semname], int(value)
            emit(eng, (lambda e, wh=wh, wv=wv: e.wait_ge(wh, wv)))

        bank_state = [[] for _ in range(8)]
        bank_rr = [0]

        def bank_acquire():
            b = bank_rr[0] % 8
            bank_rr[0] += 1
            w = bank_state[b]
            bank_state[b] = []
            return b, list(w)

        def bank_release(b, semname, value):
            bank_state[b].append((semname, value))

        def pslot(b):
            return psum[:, 512 * b:512 * (b + 1)]

        SM_RMS, SM_WFB, SM_WOB, SM_MSK, SM_EPS, SM_INVD = 0, 16, 32, 48, 64, 80

        op("sync", "q1", lambda e: e.dma_start(out=smalls[:], in_=smalls_in), k=16)
        op("sync", "q1", lambda e: e.dma_start(out=lutb[:, 0:16], in_=lut_in), k=16)
        op("sync", "q1", lambda e: e.dma_start(out=identsb[:], in_=ident_in), k=16)
        q1_init = cnt["q1"]

        def dve_warm(e):
            wb = warm[:].bitcast(BF16)
            e.memset(warm[:], 0.0)
            e.memset(wb[:, 0:256], 0.0)
            for w0, w1 in [(warm[:, 0:128], warm[:, 128:256]),
                           (wb[:, 0:128], wb[:, 128:256])]:
                e.tensor_scalar(out=w0, in0=w0, scalar1=1.0, scalar2=None,
                                op0=ALU.is_equal)
                e.scalar_tensor_tensor(out=w0, in0=w0, scalar=1.0, in1=w0,
                                       op0=ALU.mult, op1=ALU.add)
                e.tensor_copy(w1, w0)
                e.tensor_tensor(w0, w0, w1, ALU.subtract)
            e.tensor_copy(wb[:, 256:384], warm[:, 64:128].bitcast(BF16))
            e.tensor_copy(warm[:, 64:128], wb[:, 0:64])
            e.tensor_tensor(wb[:, 0:128], warm[:, 0:64].bitcast(BF16),
                            wb[:, 256:384], ALU.subtract)
            e.tensor_tensor_scan(wb[:, 0:64], wb[:, 64:128], wb[:, 128:192], 0.0,
                                 ALU.mult, ALU.add)
            e.tensor_reduce(warm[:, 128:129], warm[:, 0:64], AXX, ALU.add)
            e.reciprocal(warm[:, 130:131], warm[:, 129:130])
            e.memset(zerob[:], 0.0)
            return e.memset(warm[:], 0.0)
        dv_warm = op("vector", "dv", dve_warm)

        wait("vector", "q1", 32)

        def lut_prep(e):
            lbh = warm[:, 64:96].bitcast(BF16)
            e.tensor_copy(lbh[:, 0:16], lutb[:, 0:16])
            e.memset(warm[:, 0:64], 0.0)
            e.memset(warm[:, 0:64], 0.0)
            e.tensor_copy(luthi[:, 0:16], lbh[:, 0:16])
            e.memset(warm[:, 0:64], 0.0)
            e.memset(warm[:, 0:64], 0.0)
            llh = warm[:, 96:128].bitcast(BF16)
            e.tensor_tensor(llh[:, 0:16], lutb[:, 0:16], luthi[:, 0:16], ALU.subtract)
            e.memset(warm[:, 0:64], 0.0)
            e.memset(warm[:, 0:64], 0.0)
            return e.tensor_copy(lutlo[:, 0:16], llh[:, 0:16])
        dv_lut = op("vector", "dv", lut_prep)


        def act_warm(e):
            e.activation(out=warm[:, 0:128], in_=warm[:, 128:256], func=AF.Square)
            return e.activation(out=warm[:, 0:128], in_=warm[:, 128:256], func=AF.Sqrt)
        op("scalar", "ac", act_warm, waits=[("dv", dv_warm)])

        gchunks = []
        for arr, src in [(0, gidx_in), (1, uidx_in)]:
            for ti in range(2):
                j = 0
                while j < IT:
                    n = min(8, IT - j)
                    gchunks.append(("gu", arr, src, ti, j, n))
                    j += n
        j = 0
        while j < IT:
            n = min(4, IT - j)
            gchunks.append(("dn", 2, didx_in, 0, j, n))
            j += n
        GSLOT = [0, 4096]
        cc_arr = {}
        f32XNT = AR_XNT

        class Gather:
            def __init__(self):
                self.gi = 0
                self.k = -1
                self.loaded = 0
                self.done = False

            def _load(self, gi):
                kind, arr, srcA, ti, j0, n = gchunks[gi]
                ncols = n * 128 if kind == "gu" else n * 256
                slot = gi % 2
                idxs = AR_IDX[:, 1024 * slot:1024 * slot + ncols]
                w = []
                if gi >= 2:
                    w.append(("dv", chunk_dv[gi - 2]))

                if kind == "gu":
                    sap = srcA[ti, :, 128 * j0:128 * j0 + ncols]
                    op("gpsimd", "qg", lambda e, idxs=idxs, sap=sap:
                       e.dma_start(out=idxs, in_=sap), k=16, waits=w)
                else:
                    sap = srcA[j0:j0 + n].rearrange("i p q -> p i q")
                    dst3 = idxs.rearrange("p (i q) -> p i q", i=n)
                    op("gpsimd", "qg", lambda e, dst3=dst3, sap=sap:
                       e.dma_start(out=dst3, in_=sap), k=16, waits=w)
                chunk_ldq[gi] = cnt["qg"]
                self.loaded = gi + 1

            def step(self):
                if self.done:
                    return False
                gi = self.gi
                if gi >= len(gchunks):
                    self.done = True
                    return False
                kind, arr, srcA, ti, j0, n = gchunks[gi]
                ncols = n * 128 if kind == "gu" else n * 256
                slot = gi % 2
                base = GSLOT[slot]
                idxs = AR_IDX[:, 1024 * slot:1024 * slot + ncols]
                idxb = bfXNT[:, base:base + ncols]
                acc32 = f32XNT[:, (base + 1024) // 2:(base + 1024) // 2 + ncols]
                mask = bfXNT[:, base + 3072:base + 3072 + ncols]
                if self.k == -1:
                    while self.loaded < min(gi + 2, len(gchunks)):
                        self._load(self.loaded)

                    def cast(e, idxs=idxs, idxb=idxb, acc32=acc32):
                        e.tensor_copy(idxb, idxs)
                        return e.memset(acc32, 0.0)
                    op("vector", "dv", cast,
                       waits=[("qg", chunk_ldq[gi]), ("dv", dv_lut)])
                    self.k = 0
                    return True
                if self.k < 16:
                    k_ = self.k

                    def trip(e, idxb=idxb, acc32=acc32, mask=mask, k_=k_):
                        e.tensor_scalar(out=mask, in0=idxb, scalar1=float(k_),
                                        scalar2=None, op0=ALU.is_equal)
                        return e.scalar_tensor_tensor(
                            out=acc32, in0=mask, scalar=lutb[:, k_:k_ + 1],
                            in1=acc32, op0=ALU.mult, op1=ALU.add)
                    dvv = op("vector", "dv", trip)
                    self.k += 1
                    if self.k == 16:
                        self._acc_dv = dvv
                    return True
                ahs = op("scalar", "ac", lambda e, acc32=acc32, mask=mask:
                         e.activation(out=mask, in_=acc32, func=AF.Copy),
                         waits=[("dv", self._acc_dv)])
                dlo = op("vector", "dv", lambda e, acc32=acc32, mask=mask, idxb=idxb:
                         e.scalar_tensor_tensor(out=idxb, in0=acc32, scalar=1.0,
                                                in1=mask, op0=ALU.mult,
                                                op1=ALU.subtract),
                         waits=[("ac", ahs)])
                chunk_dv[gi] = dlo
                if kind == "gu":
                    dsth = stg[arr, ti * IT + j0:ti * IT + j0 + n, 0].rearrange(
                        "j p q -> p j q")
                    dstl = stg[arr, ti * IT + j0:ti * IT + j0 + n, 1].rearrange(
                        "j p q -> p j q")
                    srch = mask.rearrange("p (j q) -> p j q", j=n)
                    srcl = idxb.rearrange("p (j q) -> p j q", j=n)
                else:
                    dsth = stg[arr, 2 * j0:2 * (j0 + n), 0].rearrange("i p q -> p i q")
                    dstl = stg[arr, 2 * j0:2 * (j0 + n), 1].rearrange("i p q -> p i q")
                    srch = mask.rearrange("p (i q) -> p i q", i=2 * n)
                    srcl = idxb.rearrange("p (i q) -> p i q", i=2 * n)
                op("gpsimd", "qg", lambda e, d=dsth, s=srch:
                   e.dma_start(out=d, in_=s), k=16, waits=[("dv", dlo)])
                stq = op("gpsimd", "qg", lambda e, d=dstl, s=srcl:
                         e.dma_start(out=d, in_=s), k=16)
                self.gi += 1
                self.k = -1
                if gi + 1 == len(gchunks) or gchunks[gi + 1][1] != arr:
                    cc_arr[arr] = op("gpsimd", "cc", lambda e, arr=arr:
                                     e.collective_compute(
                                         "AllGather", ALU.bypass,
                                         replica_groups=[[0, 1, 2, 3, 4, 5, 6, 7]],
                                         ins=[stg[arr]], outs=[agg[arr]]),
                                     waits=[("qg", stq)])
                return True

        chunk_dv = {}
        chunk_ldq = {}
        gat = Gather()

        def drip_gather(nmax):
            while nmax > 0 and gat.step():
                nmax -= 1

        xn2hi = bfXN2[:, 0:16384]
        xn2lo = bfXN2[:, 16384:32768]
        tmpA = AR_HALL[:, 0:2048]
        tmpB = AR_TMP[:, 4096:6144]
        SUMS = stats[:, 0:16]
        SQS = stats[:, 16:32]
        MEAN = stats[:, 32:48]
        MSQ = stats[:, 48:64]
        VAR = stats[:, 64:80]
        VPE = stats[:, 80:96]
        RINV = stats[:, 96:112]
        EX2 = stats[:, 112:128]
        RSTD = stats[:, 128:144]
        M2 = stats[:, 144:160]
        M2E = stats[:, 160:176]
        R2INV = stats[:, 176:192]
        RST2 = stats[:, 192:208]
        EPS16 = smalls[:, SM_EPS:SM_EPS + 16]
        INVD16 = smalls[:, SM_INVD:SM_INVD + 16]

        last_sq_ac = 0
        d_red = 0
        for tt in range(8):
            xs = AR_TMP[:, 2048 * (tt % 2):2048 * (tt % 2) + 2048]
            w = [("dv", d_red)] if tt >= 2 else []
            xq = op("sync", "q1", lambda e, xs=xs, tt=tt:
                    e.dma_start(out=xs, in_=x_in[tt]), k=16, waits=w)
            wA = [("q1", xq)]
            if tt >= 1:
                wA.append(("dv", d_red))
            last_sq_ac = op("scalar", "ac", lambda e, xs=xs:
                            e.activation(out=tmpB, in_=xs, func=AF.Square), waits=wA)

            def reds(e, xs=xs, tt=tt):
                e.tensor_reduce(SUMS[:, tt:tt + 1], xs, AXX, ALU.add)
                return e.tensor_reduce(SQS[:, tt:tt + 1], tmpB, AXX, ALU.add)
            d_red = op("vector", "dv", reds,
                       waits=[("q1", xq), ("ac", last_sq_ac)])
            drip_gather(12)

        def widestats1(e):
            e.tensor_tensor(MEAN, SUMS, INVD16, ALU.mult)
            e.memset(warm[:], 0.0)
            e.tensor_tensor(EX2, SQS, INVD16, ALU.mult)
            e.memset(warm[:], 0.0)
            e.tensor_tensor(MSQ, MEAN, MEAN, ALU.mult)
            e.memset(warm[:], 0.0)
            e.tensor_tensor(VAR, EX2, MSQ, ALU.subtract)
            e.memset(warm[:], 0.0)
            e.tensor_tensor(VPE, VAR, EPS16, ALU.add)
            e.memset(warm[:], 0.0)
            return e.reciprocal(RINV, VPE)
        dws1 = op("vector", "dv", widestats1, waits=[("dv", d_red)])
        a_rstd = op("scalar", "ac", lambda e:
                    e.activation(out=RSTD, in_=RINV, func=AF.Sqrt),
                    waits=[("dv", dws1)])

        def widestats2(e):
            e.tensor_tensor(M2, VAR, RINV, ALU.mult)
            e.memset(warm[:], 0.0)
            e.tensor_tensor(M2E, M2, EPS16, ALU.add)
            e.memset(warm[:], 0.0)
            return e.reciprocal(R2INV, M2E)
        dws2 = op("vector", "dv", widestats2, waits=[("ac", a_rstd)])
        a_rst2 = op("scalar", "ac", lambda e:
                    e.activation(out=RST2, in_=R2INV, func=AF.Sqrt),
                    waits=[("dv", dws2)])

        pe_after_tt = {}
        spill_qg_prev = 0
        for quarter in range(4):
            for tloc in range(2):
                tt = quarter * 2 + tloc
                xs = AR_TMP[:, 2048 * (tt % 2):2048 * (tt % 2) + 2048]
                w = [("dv", d_red)]
                if tt >= 2:
                    w.append(("pe", pe_after_tt[tt - 2]))
                if tloc == 0 and quarter > 0:
                    w.append(("qg", spill_qg_prev))
                xq = op("sync", "q1", lambda e, xs=xs, tt=tt:
                        e.dma_start(out=xs, in_=x_in[tt]), k=16, waits=w)

                def normalize(e, xs=xs, tt=tt):
                    e.tensor_scalar(out=tmpB, in0=xs, scalar1=MEAN[:, tt:tt + 1],
                                    scalar2=None, op0=ALU.subtract)
                    return e.tensor_scalar(out=tmpA, in0=tmpB,
                                           scalar1=RSTD[:, tt:tt + 1], scalar2=None,
                                           op0=ALU.mult)
                wN = [("q1", xq)]
                if tt == 0:
                    wN.append(("ac", a_rst2))
                if tt >= 1:
                    wN.append(("pe", pe_after_tt[tt - 1]))
                d3 = op("vector", "dv", normalize, waits=wN)
                d5 = op("vector", "dv", lambda e, tt=tt:
                        e.tensor_scalar(out=tmpB, in0=tmpA,
                                        scalar1=RST2[:, tt:tt + 1], scalar2=None,
                                        op0=ALU.mult),
                        waits=[("dv", d3)])

                for dti in range(KT):
                    bxn, wxn = bank_acquire()
                    pexn = op("tensor", "pe", lambda e, b=bxn, dti=dti:
                              e.transpose(pslot(b)[:, 0:128],
                                          tmpA[:, 128 * dti:128 * dti + 128],
                                          identsb[:]),
                              waits=wxn + [("dv", d3), ("q1", q1_init)])
                    hq = bfBF[:, dti * 256 + 128 * tloc:dti * 256 + 128 * tloc + 128]
                    lq = bfBF[:, 4096 + dti * 256 + 128 * tloc:
                               4096 + dti * 256 + 128 * tloc + 128]
                    achi = op("scalar", "ac", lambda e, b=bxn, hq=hq:
                              e.activation(out=hq, in_=pslot(b)[:, 0:128],
                                           func=AF.Copy), waits=[("pe", pexn)])
                    dvv = op("vector", "dv", lambda e, b=bxn, hq=hq, lq=lq:
                             e.scalar_tensor_tensor(out=lq, in0=pslot(b)[:, 0:128],
                                                    scalar=1.0, in1=hq, op0=ALU.mult,
                                                    op1=ALU.subtract),
                             waits=[("pe", pexn), ("ac", achi)])
                    bank_release(bxn, "ac", achi)
                    bank_release(bxn, "dv", dvv)

                    bba, wba = bank_acquire()
                    peba = op("tensor", "pe", lambda e, b=bba, dti=dti:
                              e.transpose(pslot(b)[:, 0:128],
                                          tmpB[:, 128 * dti:128 * dti + 128],
                                          identsb[:]),
                              waits=wba + [("dv", d5)])
                    h2 = bfXN2[:, dti * 1024 + 128 * tt:dti * 1024 + 128 * tt + 128]
                    l2 = bfXN2[:, 16384 + dti * 1024 + 128 * tt:
                               16384 + dti * 1024 + 128 * tt + 128]
                    wap = smalls[:, SM_RMS + dti:SM_RMS + dti + 1]
                    aca = op("scalar", "ac", lambda e, b=bba, h2=h2, wap=wap:
                             e.activation(out=h2, in_=pslot(b)[:, 0:128], func=AF.Copy,
                                          scale=wap), waits=[("pe", peba)])

                    def evb2(e, b=bba, h2=h2, l2=l2, wap=wap):
                        return e.scalar_tensor_tensor(out=l2, in0=pslot(b)[:, 0:128],
                                                      scalar=wap, in1=h2,
                                                      op0=ALU.mult, op1=ALU.subtract)
                    dvb = op("vector", "dv", evb2, waits=[("pe", peba), ("ac", aca)])
                    bank_release(bba, "ac", aca)
                    bank_release(bba, "dv", dvb)
                pe_after_tt[tt] = cnt["pe"]
            dvq = cnt["dv"]
            for hl in range(2):
                srcs = bfBF[:, 4096 * hl:4096 * hl + 4096].rearrange(
                    "p (d t) -> p d t", d=KT)
                dsts = xnsp[hl, :, :, 256 * quarter:256 * quarter + 256].rearrange(
                    "d p t -> p d t")
                spill_qg_prev = op("gpsimd", "qg", lambda e, srcs=srcs, dsts=dsts:
                                   e.dma_start(out=dsts, in_=srcs), k=16,
                                   waits=[("dv", dvq)] if hl == 0 else [])

        phaseA_dv = cnt["dv"]
        phaseA_ac = cnt["ac"]
        xnsp_done_qg = cnt["qg"]

        FZ_F = [0, 1024]
        FZ_Z = [2048, 3072]
        bfTMP = AR_TMP[:].bitcast(BF16)
        WSLOT_AP = [bfBF[:, 4096:8192], bfBF[:, 8192:12288], bfTMP[:, 0:4096]]
        wslot_free = [[], [], []]
        w32_free = [[]]
        slot_rr = [0]

        def load_convert_w(wname, j, slot, extra_wait=()):
            w32 = AR_TMP[:, 2048:4096]
            src = wT[wname][:, :, 128 * j:128 * j + 128]
            w = list(w32_free[0]) + list(extra_wait)
            w32_free[0] = []
            lq = op("sync", "q1", lambda e, w32=w32, src=src:
                    e.dma_start(out=w32.rearrange("p (k c) -> p k c", k=KT),
                                in_=src.rearrange("k p c -> p k c")),
                    k=16, waits=w)
            sap = WSLOT_AP[slot]
            hi = sap[:, 0:2048]
            lo = sap[:, 2048:4096]
            wsl = list(wslot_free[slot])
            wslot_free[slot] = []
            ah = op("scalar", "ac", lambda e, hi=hi, w32=w32:
                    e.activation(out=hi, in_=w32, func=AF.Copy),
                    waits=[("q1", lq)] + wsl)
            dl = op("vector", "dv", lambda e, lo=lo, w32=w32, hi=hi:
                    e.tensor_tensor(lo, w32, hi, ALU.subtract),
                    waits=[("q1", lq), ("ac", ah)])
            w32_free[0] = [("ac", ah), ("dv", dl)]
            return hi, lo, ah, dl

        def mm_chain3(bank, hi, lo, xhi, xlo, th, waits):
            def chain(e):
                r = None
                for k in range(KT):
                    hk = hi[:, 128 * k:128 * k + 128]
                    lk = lo[:, 128 * k:128 * k + 128]
                    xh = xhi[:, 1024 * k + 512 * th:1024 * k + 512 * th + 512]
                    xl = xlo[:, 1024 * k + 512 * th:1024 * k + 512 * th + 512]
                    e.matmul(pslot(bank), hk, xh, start=(k == 0), stop=False)
                    e.matmul(pslot(bank), hk, xl, start=False, stop=False)
                    r = e.matmul(pslot(bank), lk, xh, start=False, stop=(k == KT - 1))
                return r
            return op("tensor", "pe", chain, waits=waits)

        phase1_ready = [("dv", phaseA_dv), ("ac", phaseA_ac), ("qg", xnsp_done_qg)]
        scan_done_dv = 0
        zgs_guard = [0, 0]

        def p1_load(j, extra=()):
            trip = {}
            for wn in ["wzg", "wz", "wf"]:
                slot = slot_rr[0] % 3
                slot_rr[0] += 1
                trip[wn] = (load_convert_w(wn, j, slot, extra_wait=extra), slot)
            return trip
        pending1 = p1_load(0, extra=phase1_ready)
        for j in range(KT):
            fb = bfBF[:, FZ_F[j % 2]:FZ_F[j % 2] + 1024]
            zb = bfBF[:, FZ_Z[j % 2]:FZ_Z[j % 2] + 1024]
            fz_guard = [("dv", scan_done_dv)] if j >= 2 else []
            per_w = {}
            for wn in ["wzg", "wz", "wf"]:
                (hi, lo, ah, dl), slot = pending1[wn]
                pe_chain = []
                for th in range(2):
                    b, bw = bank_acquire()
                    pv = mm_chain3(b, hi, lo, xn2hi, xn2lo, th,
                                   waits=bw + [("ac", ah), ("dv", dl)] +
                                   (phase1_ready if (j == 0 and wn == "wzg" and th == 0)
                                    else []))
                    pe_chain.append((b, pv))
                wslot_free[slot] = [("pe", pe_chain[-1][1])]
                per_w[wn] = pe_chain
            if j + 1 < KT:
                pending1 = p1_load(j + 1)
            for th in range(2):
                bzg, pzg = per_w["wzg"][th]
                bz, pz = per_w["wz"][th]
                bf_, pf = per_w["wf"][th]
                zgs = AR_TMP[:, 4096 + 512 * th:4096 + 512 * th + 512]
                a_zg = op("scalar", "ac", lambda e, b=bzg, zgs=zgs:
                          e.activation(out=zgs, in_=pslot(b), func=AF.Sigmoid),
                          waits=[("pe", pzg), ("dv", zgs_guard[th])])
                bank_release(bzg, "ac", a_zg)
                d_z = op("vector", "dv", lambda e, b=bz, zgs=zgs, zb=zb, th=th:
                         e.tensor_tensor(zb[:, 512 * th:512 * th + 512], pslot(b), zgs,
                                         ALU.mult),
                         waits=[("pe", pz), ("ac", a_zg)])
                bank_release(bz, "dv", d_z)
                zgs_guard[th] = d_z
                drip_gather(4)
                fbias = smalls[:, SM_WFB + j:SM_WFB + j + 1]
                a_f = op("scalar", "ac", lambda e, b=bf_, fb=fb, th=th, fbias=fbias:
                         e.activation(out=fb[:, 512 * th:512 * th + 512], in_=pslot(b),
                                      func=AF.Sigmoid, bias=fbias),
                         waits=[("pe", pf)] + (fz_guard if th == 0 else []))
                bank_release(bf_, "ac", a_f)
            hj = bfHALL[:, 1024 * j:1024 * j + 1024]
            pj = bfHALL[:, 16384 + 1024 * j:16384 + 1024 * j + 1024]
            def scans(e, fb=fb, zb=zb, hj=hj, pj=pj, j=j):
                e.tensor_tensor_scan(hj, fb, zb, 0.0, ALU.mult, ALU.add)
                return e.tensor_tensor_scan(pj, fb, zerob[:], 1.0, ALU.mult, ALU.add)
            scan_done_dv = op("vector", "dv", scans, waits=[("ac", cnt["ac"])])
            drip_gather(10)

        all_scans_dv = scan_done_dv

        def p2_load(j):
            trip = {}
            for wn in ["wga", "wgb"]:
                slot = slot_rr[0] % 3
                slot_rr[0] += 1
                trip[wn] = (load_convert_w(wn, j, slot), slot)
            return trip
        pending2 = p2_load(0)

        bsrc = bfHALL[:, 0:16384].rearrange(
            "p (j t) -> p j t", j=KT)[:, :, 1023:1024].squeeze(2)
        all_scans_dv = op("vector", "dv", lambda e, bsrc=bsrc:
                          e.tensor_copy(bndsb[:, 0:16], bsrc))
        bq = op("gpsimd", "qg", lambda e:
                e.dma_start(out=bnd_in, in_=bndsb[:, 0:16]),
                k=16, waits=[("dv", all_scans_dv)])
        cc1 = op("gpsimd", "cc", lambda e: e.collective_compute(
            "AllGather", ALU.bypass,
            replica_groups=[[0, 1], [2, 3], [4, 5], [6, 7]],
            ins=[bnd_in], outs=[bnd_out]), waits=[("qg", bq)])
        hbq = op("gpsimd", "qg", lambda e: e.dma_start(out=bndsb[:, 64:80],
                                                       in_=bnd_out[0]),
                 k=16, waits=[("cc", cc1)])

        corr_holder = [0]

        def emit_boundary_tail():
            hbm_ac = op("scalar", "ac", lambda e:
                        e.activation(out=bndsb[:, 128:144], in_=bndsb[:, 64:80],
                                     func=AF.Copy,
                                     scale=smalls[:, SM_MSK:SM_MSK + 1]),
                        waits=[("qg", hbq)])
            hbm_dv = op("vector", "dv", lambda e: e.memset(warm[:], 0.0),
                        waits=[("ac", hbm_ac)])

            def corrections(e):
                r = None
                for j in range(KT):
                    hj = bfHALL[:, 1024 * j:1024 * j + 1024]
                    pj = bfHALL[:, 16384 + 1024 * j:16384 + 1024 * j + 1024]
                    r = e.scalar_tensor_tensor(out=hj, in0=pj,
                                               scalar=bndsb[:, 128 + j:129 + j],
                                               in1=hj, op0=ALU.mult, op1=ALU.add)
                return r
            corr_holder[0] = op("vector", "dv", corrections, waits=[("dv", hbm_dv)])

        gas_guard = list(zgs_guard)
        for j in range(KT):
            per_w = {}
            for wn in ["wga", "wgb"]:
                (hi, lo, ah, dl), slot = pending2[wn]
                pe_chain = []
                for th in range(2):
                    b, bw = bank_acquire()
                    pv = mm_chain3(b, hi, lo, xn2hi, xn2lo, th,
                                   waits=bw + [("ac", ah), ("dv", dl)])
                    pe_chain.append((b, pv))
                wslot_free[slot] = [("pe", pe_chain[-1][1])]
                per_w[wn] = pe_chain
            if j + 1 < KT:
                pending2 = p2_load(j + 1)
            ogb = bfBF[:, FZ_F[j % 2]:FZ_F[j % 2] + 1024]
            for th in range(2):
                bga, pga = per_w["wga"][th]
                bgb, pgb = per_w["wgb"][th]
                gas = AR_TMP[:, 4096 + 512 * th:4096 + 512 * th + 512]
                a_ga = op("scalar", "ac", lambda e, b=bga, gas=gas:
                          e.activation(out=gas, in_=pslot(b), func=AF.Silu),
                          waits=[("pe", pga), ("dv", gas_guard[th])])
                bank_release(bga, "ac", a_ga)
                d_og = op("vector", "dv", lambda e, b=bgb, gas=gas, ogb=ogb, th=th:
                          e.tensor_tensor(ogb[:, 512 * th:512 * th + 512], pslot(b),
                                          gas, ALU.mult),
                          waits=[("pe", pgb), ("ac", a_ga)])
                bank_release(bgb, "dv", d_og)
                gas_guard[th] = d_og
                drip_gather(12)
            if j == 0:
                emit_boundary_tail()
            if j >= 1:
                jp = j - 1
                gp = bfHALL[:, 16384 + 1024 * jp:16384 + 1024 * jp + 1024]
                hp = bfHALL[:, 1024 * jp:1024 * jp + 1024]
                ogp = bfBF[:, FZ_F[jp % 2]:FZ_F[jp % 2] + 1024]
                op("vector", "dv", lambda e, gp=gp, hp=hp, ogp=ogp:
                   e.tensor_tensor(gp, ogp, hp, ALU.mult),
                   waits=[("dv", corr_holder[0])])
        op("vector", "dv", lambda e: e.memset(warm[:], 0.0))
        gp = bfHALL[:, 16384 + 1024 * 15:16384 + 1024 * 15 + 1024]
        hp = bfHALL[:, 1024 * 15:1024 * 15 + 1024]
        ogp = bfBF[:, FZ_F[15 % 2]:FZ_F[15 % 2] + 1024]
        op("vector", "dv", lambda e, gp=gp, hp=hp, ogp=ogp:
           e.tensor_tensor(gp, ogp, hp, ALU.mult))
        all_gated_dv = cnt["dv"]
        drip_gather(len(gchunks) * 20)

        pe_before_wout = cnt["pe"]
        guslot_free = [[], []]
        gu_loads = {}

        def emit_gu_load(i, arr, extra=()):
            slot = (2 * i + arr) % 2
            lw = list(guslot_free[slot]) + list(extra)
            guslot_free[slot] = []
            dstW = bfXNT[:, 4096 * slot:4096 * slot + 4096]
            dW5 = dstW.rearrange("p (r ti hl c) -> p r ti hl c", r=8, ti=2, hl=2)
            lq = None
            for ti in range(2):
                for hl in range(2):
                    srcW = agg[arr, :, i + IT * ti, hl].rearrange("r p c -> p r c")
                    dstWt = dW5[:, :, ti, hl]
                    lq = op("gpsimd", "qg",
                            lambda e, dstWt=dstWt, srcW=srcW:
                            e.dma_start(out=dstWt, in_=srcW),
                            k=16,
                            waits=(lw + [("cc", cc_arr[arr]),
                                         ("cc", cc_arr[2])])
                            if (ti == 0 and hl == 0) else [])
            gu_loads[(i, arr)] = (slot, lq)

        gated = bfHALL[:, 16384:32768]
        emit_gu_load(0, 0)
        emit_gu_load(0, 1)
        ysp_guard = [0, 0]

        def pw_load(j):
            slot = slot_rr[0] % 3
            slot_rr[0] += 1
            return load_convert_w("wo", j, slot), slot
        pendw = pw_load(0)
        for j in range(KT):
            (hi, lo, ah, dl), slot = pendw
            for th in range(2):
                b, bw = bank_acquire()

                def chain2(e, b=b, hi=hi, lo=lo, th=th):
                    r = None
                    for k in range(KT):
                        hk = hi[:, 128 * k:128 * k + 128]
                        lk = lo[:, 128 * k:128 * k + 128]
                        g = gated[:, 1024 * k + 512 * th:1024 * k + 512 * th + 512]
                        e.matmul(pslot(b), hk, g, start=(k == 0), stop=False)
                        r = e.matmul(pslot(b), lk, g, start=False, stop=(k == KT - 1))
                    return r
                pv = op("tensor", "pe", chain2,
                        waits=bw + [("ac", ah), ("dv", dl)] +
                        ([("dv", all_gated_dv)] if (j == 0 and th == 0) else []))
                if th == 1:
                    wslot_free[slot] = [("pe", pv)]
                    if j + 1 < KT:
                        pendw = pw_load(j + 1)
                ybuf = AR_TMP[:, 5120 + 512 * th:5120 + 512 * th + 512]
                bias = smalls[:, SM_WOB + j:SM_WOB + j + 1]
                av = op("vector", "dv", lambda e, b=b, ybuf=ybuf, bias=bias:
                        e.tensor_scalar(out=ybuf, in0=pslot(b), scalar1=bias,
                                        scalar2=None, op0=ALU.add),
                        waits=[("pe", pv), ("qg", ysp_guard[th])])
                bank_release(b, "dv", av)
                ysp_guard[th] = op("gpsimd", "qg", lambda e, ybuf=ybuf, j=j, th=th:
                                   e.dma_start(out=ysp[j, :, 512 * th:512 * th + 512],
                                               in_=ybuf),
                                   k=16, waits=[("dv", av)])
        ysp_qg = cnt["qg"]
        wout_pe = cnt["pe"]

        xn_rel = 0
        for hl in range(2):
            for dti in range(KT):
                if hl == 0:
                    dst = bfHALL[:, 1024 * dti:1024 * dti + 1024]
                    wlist = [("dv", all_gated_dv)] if dti == 0 else []
                else:
                    dst = bfXN2[:, 1024 * dti:1024 * dti + 1024]
                    wlist = [("pe", pe_before_wout)] if dti == 0 else []
                xn_rel = op("sync", "q1", lambda e, dst=dst, hl=hl, dti=dti:
                            e.dma_start(out=dst, in_=xnsp[hl, dti]), k=16, waits=wlist)
        xnhi_sb = bfHALL[:, 0:16384]
        xnlo_sb = bfXN2[:, 0:16384]

        def ycol(j, th):
            if j < 8:
                return AR_XN2[:, 8192 + 1024 * j + 512 * th:
                              8192 + 1024 * j + 512 * th + 512]
            return AR_HALL[:, 8192 + 1024 * (j - 8) + 512 * th:
                           8192 + 1024 * (j - 8) + 512 * th + 512]

        def yfull(j):
            if j < 8:
                return AR_XN2[:, 8192 + 1024 * j:8192 + 1024 * j + 1024]
            return AR_HALL[:, 8192 + 1024 * (j - 8):8192 + 1024 * (j - 8) + 1024]

        dn_free = [[], [], []]
        chunks = []
        i0 = 0
        while i0 < IT:
            chunks.append((i0, min(DCHUNK, IT - i0)))
            i0 += DCHUNK

        first_gu = [True]
        h_guard = {}
        gas_guard2 = list(gas_guard)
        htmp_guard = [dict(th=0), None]
        htmp_first = [True, True]
        dve_after_chunk = 0
        for ci, (c0, cn) in enumerate(chunks):
            gate_pair = up_pair = None
            h_ready = []
            for ii in range(cn):
                i = c0 + ii
                pair = {}
                for arr in (0, 1):
                    if (i, arr) not in gu_loads:
                        emit_gu_load(i, arr)
                    slot, lq = gu_loads.pop((i, arr))
                    dstW = bfXNT[:, 4096 * slot:4096 * slot + 4096]
                    wsl = dstW
                    pe_pair = []
                    for th in range(2):
                        b, bw = bank_acquire()

                        def chaing(e, b=b, wsl=wsl, th=th):
                            r = None
                            for k in range(KT):
                                hk = wsl[:, 256 * k:256 * k + 128]
                                lk = wsl[:, 256 * k + 128:256 * k + 256]
                                xh = xnhi_sb[:, 1024 * k + 512 * th:
                                             1024 * k + 512 * th + 512]
                                xl = xnlo_sb[:, 1024 * k + 512 * th:
                                             1024 * k + 512 * th + 512]
                                e.matmul(pslot(b), hk, xh, start=(k == 0), stop=False)
                                e.matmul(pslot(b), hk, xl, start=False, stop=False)
                                r = e.matmul(pslot(b), lk, xh, start=False,
                                             stop=(k == KT - 1))
                            return r
                        pv = op("tensor", "pe", chaing,
                                waits=bw + [("qg", lq)] +
                                ([("q1", xn_rel)] if (ci == 0 and ii == 0 and th == 0
                                                      and arr == 0) else []))
                        pe_pair.append((b, pv))
                    guslot_free[slot] = [("pe", pe_pair[1][1])]
                    pair[arr] = pe_pair
                gate_pair, up_pair = pair[0], pair[1]
                for th in range(2):
                    bg, pg = gate_pair[th]
                    bu, pu = up_pair[th]
                    gas = AR_TMP[:, 4096 + 512 * th:4096 + 512 * th + 512]
                    ag = op("scalar", "ac", lambda e, b=bg, gas=gas:
                            e.activation(out=gas, in_=pslot(b), func=AF.Silu),
                            waits=[("pe", pg), ("dv", gas_guard2[th])])
                    bank_release(bg, "ac", ag)
                    htmp = AR_TMP[:, 5120 + 512 * th:5120 + 512 * th + 512]
                    hhi = bfBF[:, 4096 + 2048 * ii + 512 * th:
                               4096 + 2048 * ii + 512 * th + 512]
                    hlo = bfBF[:, 4096 + 2048 * ii + 1024 + 512 * th:
                               4096 + 2048 * ii + 1024 + 512 * th + 512]
                    hw = [("pe", pu), ("ac", ag)]
                    if (ii, th) in h_guard:
                        hw.append(("pe", h_guard[(ii, th)]))
                    if htmp_first[th]:
                        hw.append(("qg", ysp_guard[th]))
                        htmp_first[th] = False

                    dm = op("vector", "dv", lambda e, b=bu, gas=gas, htmp=htmp:
                            e.tensor_tensor(htmp, pslot(b), gas, ALU.mult), waits=hw)
                    ah2 = op("scalar", "ac", lambda e, htmp=htmp, hhi=hhi:
                             e.activation(out=hhi, in_=htmp, func=AF.Copy),
                             waits=[("dv", dm)])
                    dh = op("vector", "dv", lambda e, htmp=htmp, hhi=hhi, hlo=hlo:
                            e.scalar_tensor_tensor(out=hlo, in0=htmp, scalar=1.0,
                                                   in1=hhi, op0=ALU.mult,
                                                   op1=ALU.subtract),
                            waits=[("ac", ah2)])
                    bank_release(bu, "dv", dm)
                    gas_guard2[th] = dh
                    h_ready.append(dh)
            h_all_ready = max(h_ready)
            for j in range(KT):
                dslot = j % 3
                dw = list(dn_free[dslot])
                dn_free[dslot] = []
                dsb = bfBF[:, 1024 * dslot:1024 * dslot + 256 * cn]
                dsb4 = dsb.rearrange("p (i hl c) -> p i hl c", i=cn, hl=2)
                for hl in range(2):
                    srcD = agg[2, j // 2, 2 * c0 + (j % 2):2 * (c0 + cn):2, hl]
                    dwx = (dw + ([("cc", cc_arr[2]), ("dv", all_gated_dv)]
                                 if ci == 0 else [])) if hl == 0 else []
                    lq = op("gpsimd", "qg",
                            lambda e, dsb4=dsb4, srcD=srcD, hl=hl:
                            e.dma_start(out=dsb4[:, :, hl],
                                        in_=srcD.rearrange("i p c -> p i c")),
                            k=16, waits=dwx)
                last_pv = 0
                for th in range(2):
                    b, bw = bank_acquire()

                    def chaind(e, b=b, dsb=dsb, th=th, cn=cn):
                        r = None
                        for ii in range(cn):
                            hk = dsb[:, 256 * ii:256 * ii + 128]
                            lk = dsb[:, 256 * ii + 128:256 * ii + 256]
                            xh = bfBF[:, 4096 + 2048 * ii + 512 * th:
                                      4096 + 2048 * ii + 512 * th + 512]
                            xl = bfBF[:, 4096 + 2048 * ii + 1024 + 512 * th:
                                      4096 + 2048 * ii + 1024 + 512 * th + 512]
                            e.matmul(pslot(b), hk, xh, start=(ii == 0), stop=False)
                            e.matmul(pslot(b), hk, xl, start=False, stop=False)
                            r = e.matmul(pslot(b), lk, xh, start=False,
                                         stop=(ii == cn - 1))
                        return r
                    pv = op("tensor", "pe", chaind,
                            waits=bw + [("qg", lq), ("dv", h_all_ready)])
                    last_pv = pv
                    yc = ycol(j, th)
                    if ci == 0:
                        dy = op("vector", "dv", lambda e, b=b, yc=yc:
                                e.tensor_copy(yc, pslot(b)), waits=[("pe", pv)])
                    else:
                        dy = op("vector", "dv", lambda e, b=b, yc=yc:
                                e.scalar_tensor_tensor(out=yc, in0=pslot(b),
                                                       scalar=1.0, in1=yc,
                                                       op0=ALU.mult, op1=ALU.add),
                                waits=[("pe", pv)])
                    bank_release(b, "dv", dy)
                dn_free[dslot] = [("pe", last_pv)]
            for ii in range(DCHUNK):
                for th in range(2):
                    h_guard[(ii, th)] = cnt["pe"]
            dve_after_chunk = cnt["dv"]
        ffn_dv = cnt["dv"]

        g_buf = AR_TMP[:, 0:1024]
        r_done = {}
        prev_r = ffn_dv
        for j in range(KT):
            ya = AR_TMP[:, 1024 + 1024 * (j % 2):1024 + 1024 * (j % 2) + 1024]
            w = [("dv", r_done[j - 2])] if j >= 2 else []
            yq = op("sync", "q1", lambda e, ya=ya, j=j:
                    e.dma_start(out=ya, in_=ysp[j]), k=16,
                    waits=w + ([("qg", ysp_qg)] if j == 0 else []))
            aj = op("scalar", "ac", lambda e, j=j:
                    e.activation(out=g_buf, in_=yfull(j), func=AF.Silu),
                    waits=[("dv", ffn_dv if j == 0 else prev_r)])

            def rops(e, ya=ya, j=j):
                yj = yfull(j)
                e.scalar_tensor_tensor(out=ya, in0=xnhi_sb[:, 1024 * j:1024 * j + 1024],
                                       scalar=1.0, in1=ya, op0=ALU.mult, op1=ALU.add)
                e.scalar_tensor_tensor(out=ya, in0=xnlo_sb[:, 1024 * j:1024 * j + 1024],
                                       scalar=1.0, in1=ya, op0=ALU.mult, op1=ALU.add)
                return e.scalar_tensor_tensor(out=yj, in0=g_buf, scalar=1.0, in1=ya,
                                              op0=ALU.mult, op1=ALU.mult)
            prev_r = op("vector", "dv", rops,
                        waits=[("q1", yq), ("ac", aj), ("dv", ffn_dv)])
            r_done[j] = prev_r
        all_r_dv = prev_r

        out_q_map = {}
        prev_out_dv = all_r_dv
        for tt in range(8):
            xb = AR_TMP[:, 3072:5120]
            w = [("dv", prev_out_dv)] if tt >= 1 else [("dv", all_r_dv)]
            xq = op("sync", "q1", lambda e, xb=xb, tt=tt:
                    e.dma_start(out=xb, in_=x_in[tt]), k=16, waits=w)
            outsb = AR_XN2[:, 2048 * (tt % 2):2048 * (tt % 2) + 2048]
            grp_dv = 0
            for g4 in range(4):
                b, bw = bank_acquire()

                def tchain(e, b=b, tt=tt, g4=g4):
                    r = None
                    for jj in range(4):
                        j = 4 * g4 + jj
                        r = e.transpose(
                            pslot(b)[:, 128 * jj:128 * jj + 128],
                            yfull(j)[:, 128 * tt:128 * tt + 128],
                            identsb[:])
                    return r
                pv = op("tensor", "pe", tchain, waits=bw + [("dv", all_r_dv)])
                wv = [("pe", pv), ("q1", xq)]
                if tt >= 2 and g4 == 0:
                    wv.append(("qg", out_q_map[tt - 2]))
                dv = op("vector", "dv", lambda e, b=b, outsb=outsb, xb=xb, g4=g4:
                        e.scalar_tensor_tensor(out=outsb[:, 512 * g4:512 * g4 + 512],
                                               in0=pslot(b), scalar=1.0,
                                               in1=xb[:, 512 * g4:512 * g4 + 512],
                                               op0=ALU.mult, op1=ALU.add),
                        waits=wv)
                bank_release(b, "dv", dv)
                grp_dv = dv
            prev_out_dv = grp_dv
            out_q_map[tt] = op("gpsimd", "qg", lambda e, outsb=outsb, tt=tt:
                               e.dma_start(out=out_dram[tt], in_=outsb),
                               k=16, waits=[("dv", grp_dv)])
        wait("gpsimd", "qg", cnt["qg"])
        wait("sync", "q1", cnt["q1"])

        with nc.Block() as block:
            @block.sync
            def _(e):
                for f in prog["sync"]:
                    f(e)

            @block.gpsimd
            def _(e):
                for f in prog["gpsimd"]:
                    f(e)

            @block.vector
            def _(e):
                for f in prog["vector"]:
                    f(e)

            @block.scalar
            def _(e):
                for f in prog["scalar"]:
                    f(e)

            @block.tensor
            def _(e):
                for f in prog["tensor"]:
                    f(e)
    return nc


def kernel(L0_attn_x, L0_attn_rms_w, L0_attn_Wf_w, L0_attn_Wf_b, L0_attn_Wzg_w,
           L0_attn_Wz_w, L0_attn_Wga_w, L0_attn_Wgb_w, L0_attn_Wout_w, L0_attn_Wout_b,
           L0_ffn_lut, L0_ffn_gate_idx, L0_ffn_up_idx, L0_ffn_down_idx):
    x = np.ascontiguousarray(np.asarray(L0_attn_x, dtype=np.float32))
    B, Tfull, Dm = x.shape
    xs = x.reshape(B, 2, T, Dm)

    def prepT(w):
        return np.ascontiguousarray(np.asarray(w, np.float32).T).reshape(KT, 128, D)

    wfT = prepT(L0_attn_Wf_w)
    wzgT = prepT(L0_attn_Wzg_w)
    wzT = prepT(L0_attn_Wz_w)
    wgaT = prepT(L0_attn_Wga_w)
    wgbT = prepT(L0_attn_Wgb_w)
    woT = prepT(L0_attn_Wout_w)

    rms_w = np.asarray(L0_attn_rms_w, np.float32)
    wf_b = np.asarray(L0_attn_Wf_b, np.float32)
    wout_b = np.asarray(L0_attn_Wout_b, np.float32)
    lut = np.asarray(L0_ffn_lut, np.float32)
    gidxT = np.ascontiguousarray(np.asarray(L0_ffn_gate_idx, np.int32).T)
    uidxT = np.ascontiguousarray(np.asarray(L0_ffn_up_idx, np.int32).T)
    didxT = np.ascontiguousarray(np.asarray(L0_ffn_down_idx, np.int32).T)

    ident = np.eye(128, dtype=np.float32)
    lut128 = np.tile(lut.reshape(1, 16), (128, 1)).astype(np.float32)

    nc = build_nc()
    in_maps = []
    for c in range(8):
        b, half = c // 2, c % 2
        smalls = np.zeros((128, 128), np.float32)
        smalls[:, 0:16] = rms_w.reshape(16, 128).T
        smalls[:, 16:32] = wf_b.reshape(16, 128).T
        smalls[:, 32:48] = wout_b.reshape(16, 128).T
        smalls[:, 48] = 1.0 if half == 1 else 0.0
        smalls[:, 64:80] = EPS
        smalls[:, 80:96] = 1.0 / D
        gs = np.ascontiguousarray(gidxT[256 * c:256 * c + 256]).reshape(2, 128, INTER)
        us = np.ascontiguousarray(uidxT[256 * c:256 * c + 256]).reshape(2, 128, INTER)
        ds = np.ascontiguousarray(didxT[:, 256 * c:256 * c + 256]).reshape(IT, 128, 256)
        in_maps.append({
            "x_in": np.ascontiguousarray(xs[b, half]).reshape(8, 128, D),
            "wfT": wfT, "wzgT": wzgT, "wzT": wzT, "wgaT": wgaT, "wgbT": wgbT,
            "woT": woT,
            "smalls": smalls, "lut128": lut128, "ident": ident,
            "gidx": gs, "uidx": us, "didx": ds,
        })
    res = run_bass_kernel_spmd(nc, in_maps, core_ids=list(range(8)))
    out = np.empty((B, Tfull, Dm), np.float32)
    for c in range(8):
        b, half = c // 2, c % 2
        out[b, T * half:T * half + T] = res.results[c]["out"].reshape(T, Dm)
    return out
